# version 01
# baseline (speedup 1.0000x reference)
import numpy as np
import ml_dtypes
from contextlib import ExitStack
import concourse.bass as bass
import concourse.mybir as mybir
from concourse.bass_utils import run_bass_kernel_spmd

F32 = mybir.dt.float32
BF16 = mybir.dt.bfloat16
I8 = mybir.dt.int8
AF = mybir.ActivationFunctionType
ALU = mybir.AluOpType
AX = mybir.AxisListType
NPBF16 = ml_dtypes.bfloat16

ENGS = ["pe", "act", "dve", "pool", "sp"]
EMAP = {"pe": "tensor", "act": "scalar", "dve": "vector", "pool": "gpsimd", "sp": "sync"}
DMA_QS = {"sp": 8, "pool": 8, "act": 4}

NCORE = 8
E = 2048
T = 16384
S = 8192
NTOK = T // NCORE
NT = NTOK // 128


class Prog:

    def __init__(self, nc, stack):
        self.nc = nc
        self.sems = {e: stack.enter_context(nc.semaphore(name=f"s_{e}")) for e in ENGS}
        self.dsems = {q: [stack.enter_context(nc.semaphore(name=f"d_{q}{i}"))
                          for i in range(n)] for q, n in DMA_QS.items()}
        self.cnt = {e: 0 for e in ENGS}
        self.dcnt = {q: [0] * n for q, n in DMA_QS.items()}
        self.drr = {q: 0 for q in DMA_QS}
        self.clock = {e: {} for e in ENGS}
        self.snap = {}
        self.writers = {}
        self.readers = {}
        self.ops = {e: [] for e in ENGS}

    def _semobj(self, key):
        if isinstance(key, str):
            return self.sems[key]
        q, _, si = key
        return self.dsems[q][si]

    def _need(self, reads, writes):
        need = {}

        def add(m):
            k, v = m
            if need.get(k, 0) < v:
                need[k] = v
        for b in reads:
            for m in self.writers.get(b, ()):
                add(m)
        for b in writes:
            for m in self.writers.get(b, ()):
                add(m)
            for m in self.readers.get(b, ()):
                add(m)
        return need

    SELF_GAP = 4

    def _waits(self, e, need):
        ck = self.clock[e]
        waits = []
        for k, v in sorted(need.items(), key=lambda kv: str(kv[0])):
            if k == e:
                if v > self.cnt[e] - self.SELF_GAP and ck.get(k, 0) < v:
                    waits.append((k, v))
                    ck[k] = v
                continue
            if ck.get(k, 0) >= v:
                continue
            waits.append((k, v))
            sn = self.snap.get((k, v))
            if sn:
                for k2, v2 in sn.items():
                    if ck.get(k2, 0) < v2:
                        ck[k2] = v2
            if ck.get(k, 0) < v:
                ck[k] = v
        return waits

    def _record(self, m, reads, writes):
        for b in writes:
            self.writers[b] = [m]
            self.readers[b] = []
        for b in reads:
            self.readers.setdefault(b, []).append(m)

    def emit(self, e, reads, writes, fn):
        waits = self._waits(e, self._need(reads, writes))
        self.cnt[e] += 1
        m = (e, self.cnt[e])
        sn = dict(self.clock[e])
        sn[e] = self.cnt[e]
        self.snap[m] = sn
        self.ops[e].append((waits, fn, (e, 1)))
        self._record(m, reads, writes)
        return m

    def dma(self, q, reads, writes, fn):
        need = self._need(reads, writes)
        si = self.drr[q]
        self.drr[q] = (si + 1) % DMA_QS[q]
        key = (q, "d", si)
        prev = self.dcnt[q][si]
        if prev > 0 and need.get(key, 0) < 16 * prev:
            need[key] = 16 * prev
        waits = self._waits(q, need)
        self.dcnt[q][si] += 1
        m = (key, 16 * self.dcnt[q][si])
        sn = dict(self.clock[q])
        sn[key] = m[1]
        self.snap[m] = sn
        self.ops[q].append((waits, fn, (key, 16)))
        self._record(m, reads, writes)
        return m

    def finish(self):
        waits = []
        for e in ENGS:
            if e != "sp" and self.cnt[e] > 0:
                waits.append((e, self.cnt[e]))
        for q, cnts in self.dcnt.items():
            for si, c in enumerate(cnts):
                if c > 0:
                    waits.append(((q, "d", si), 16 * c))
        self.ops["sp"].append((waits, None, None))

    def build(self):
        self.finish()
        with self.nc.Block() as block:
            for e in ENGS:
                def mk(e):
                    def body(eng):
                        for waits, fn, inc in self.ops[e]:
                            for (k, v) in waits:
                                eng.wait_ge(self._semobj(k), v)
                            if fn is not None:
                                inst = fn(eng)
                                inst.then_inc(self._semobj(inc[0]), inc[1])
                    return body
                getattr(block, EMAP[e])(mk(e))


def new_nc():
    return bass.Bass("TRN2", target_bir_lowering=False)


def run_launch(builder, in_maps, **kw):
    nc = new_nc()
    with ExitStack() as stack:
        p = Prog(nc, stack)
        builder(nc, p, stack)
        p.build()
    res = run_bass_kernel_spmd(nc, in_maps, core_ids=list(range(NCORE)), **kw)
    return res.results



def k_load_wcast(p, q, dst, dname, src, KT, N, ndesc=None):
    for kt in range(KT):
        p.dma(q, [], [(dname, kt)],
              lambda e, kt=kt: e.dma_start(out=dst[:, kt * N:(kt + 1) * N],
                                           in_=src[kt * 128:(kt + 1) * 128, :],
                                           max_dma_last_dim=ndesc))


def k_ln(p, xk, x_ap, out_ap, outk, scr, D, extra_reads=()):
    sq, s1, nm, sd, rs, eps = scr["sq"], scr["s1"], scr["nm"], scr["sd"], scr["rs"], scr["eps"]
    s2 = scr["s2"]
    er = list(extra_reads)
    p.emit("dve", [xk] + er, [("lnst", 0)], lambda e: e.reduce_sum(s1[:, :], x_ap, axis=AX.X))
    p.emit("dve", [("lnst", 0)], [("lnst", 1)],
           lambda e: e.tensor_scalar(out=nm[:, :], in0=s1[:, :], scalar1=-1.0 / D, scalar2=None, op0=ALU.mult))
    p.emit("act", [xk, ("lnst", 1)] + er, [("lnst", 2)],
           lambda e: e.activation(out=sq[:, :], in_=x_ap, func=AF.Square, bias=nm[:, :], scale=1.0))
    p.emit("dve", [("lnst", 2)], [("lnst", 3)], lambda e: e.reduce_sum(s2[:, :], sq[:, :], axis=AX.X))
    p.emit("act", [("lnst", 3), "eps"], [("lnst", 4)],
           lambda e: e.activation(out=sd[:, :], in_=s2[:, :], func=AF.Sqrt, bias=eps[:, :], scale=1.0 / D))
    p.emit("dve", [("lnst", 4)], [("lnst", 5)], lambda e: e.reciprocal(out=rs[:, :], in_=sd[:, :]))
    p.emit("dve", [xk, ("lnst", 1), ("lnst", 5)] + er, [outk],
           lambda e: e.tensor_scalar(out=out_ap, in0=x_ap, scalar1=nm[:, :], scalar2=rs[:, :],
                                     op0=ALU.add, op1=ALU.mult))


def k_transpose16(p, src, srck, dst, dstname, ident, pbt, n, dst_off=0):
    for i in range(n):
        pb, pbk = pbt[i % 2]
        p.emit("pe", [srck if not isinstance(srck, list) else srck[i], "ident"], [pbk],
               lambda e, i=i, pb=pb: e.transpose(pb[:, :128], src[:, i * 128:(i + 1) * 128], ident[:, :]))
        p.emit("dve", [pbk], [(dstname, dst_off + i)],
               lambda e, i=i, pb=pb: e.tensor_copy(out=dst[:, (dst_off + i) * 128:(dst_off + i + 1) * 128],
                                                   in_=pb[:, :128]))



A1_DEBUG = False

def build_a1(nc, p, stack):
    sb = stack.enter_context
    x = nc.dram_tensor("x", [NTOK, E], F32, kind="ExternalInput").ap()
    Ws = {w: nc.dram_tensor(w, [E, E], F32, kind="ExternalInput").ap() for w in ["Wq", "Wk", "Wv"]}
    Bs = {b: nc.dram_tensor(b, [E], F32, kind="ExternalInput").ap() for b in ["bq", "bk", "bv"]}
    cosr = nc.dram_tensor("cosr", [NTOK, E], F32, kind="ExternalInput").ap()
    sinr = nc.dram_tensor("sinr", [NTOK, E], F32, kind="ExternalInput").ap()
    ident = nc.dram_tensor("ident", [128, 128], BF16, kind="ExternalInput").ap()
    ones1 = nc.dram_tensor("ones1", [1, 128], BF16, kind="ExternalInput").ap()
    outs = {w: nc.dram_tensor(w, [NTOK, E], BF16, kind="ExternalOutput").ap()
            for w in ["qo", "ko", "vo"]}
    if A1_DEBUG:
        lno = nc.dram_tensor("lno", [NTOK, E], BF16, kind="ExternalOutput").ap()
        lnTo = nc.dram_tensor("lnTo", [128, NT * 16 * 128], BF16, kind="ExternalOutput").ap()
        stto = nc.dram_tensor("stto", [NTOK, 8], F32, kind="ExternalOutput").ap()
        stt_sb = sb(nc.sbuf_tensor("stt_sb", [128, 8], F32))

    x_t = [sb(nc.sbuf_tensor(f"x_t{i}", [128, E], F32)) for i in range(2)]
    ln_b = [sb(nc.sbuf_tensor(f"ln_b{i}", [128, E], BF16)) for i in range(2)]
    lnT = sb(nc.sbuf_tensor("lnT", [128, NT * 16 * 128], BF16))
    w_sb = sb(nc.sbuf_tensor("w_sb", [128, 16 * E], BF16))
    b_b = sb(nc.sbuf_tensor("b_b", [1, E], BF16))
    on1 = sb(nc.sbuf_tensor("on1", [1, 128], BF16))
    id_b = sb(nc.sbuf_tensor("id_b", [128, 128], BF16))
    cos_t = [sb(nc.sbuf_tensor("cos_t", [128, E], F32))] * 2
    sin_t = [sb(nc.sbuf_tensor("sin_t", [128, E], F32))] * 2
    out_f = sb(nc.sbuf_tensor("out_f", [128, E], F32))
    out_b = [sb(nc.sbuf_tensor(f"out_b{i}", [128, E], BF16)) for i in range(2)]
    r1 = sb(nc.sbuf_tensor("r1", [128, E // 2], F32))
    r2 = sb(nc.sbuf_tensor("r2", [128, E // 2], F32))
    scr = {"sq": sb(nc.sbuf_tensor("sq", [128, E], F32)),
           "s1": sb(nc.sbuf_tensor("s1", [128, 1], F32)),
           "s2": sb(nc.sbuf_tensor("s2", [128, 1], F32)),
           "nm": sb(nc.sbuf_tensor("nm", [128, 1], F32)),
           "sd": sb(nc.sbuf_tensor("sd", [128, 1], F32)),
           "rs": sb(nc.sbuf_tensor("rs", [128, 1], F32)),
           "eps": sb(nc.sbuf_tensor("eps", [128, 1], F32))}
    pmm = [(sb(nc.psum_tensor(f"pmm{i}", [128, 512], F32)), f"pmm{i}") for i in range(2)]
    pbt = [(sb(nc.psum_tensor(f"pbt{i}", [128, 1024], BF16)), f"pbt{i}") for i in range(2)]

    p.emit("dve", [], ["eps"], lambda e: e.memset(scr["eps"][:, :], 1e-5))
    p.dma("sp", [], ["ident"], lambda e: e.dma_start(out=id_b[:, :], in_=ident[:, :]))
    p.dma("sp", [], ["on1"], lambda e: e.dma_start(out=on1[:, :], in_=ones1[:, :]))

    for tt in range(NT):
        xb = x_t[tt % 2]
        xk = ("x_t", tt % 2)
        p.dma("sp", [], [xk], lambda e, tt=tt, xb=xb: e.dma_start(out=xb[:, :], in_=x[tt * 128:(tt + 1) * 128, :]))
        lb = ln_b[tt % 2]
        lk = ("ln_b", tt % 2)
        k_ln(p, xk, xb[:, :], lb[:, :], lk, scr, E)
        if A1_DEBUG:
            p.dma("pool", [lk], [], lambda e, tt=tt, lb=lb: e.dma_start(out=lno[tt * 128:(tt + 1) * 128, :], in_=lb[:, :]))
            for ci, (tile, kk) in enumerate([(scr["s1"], ("lnst", 0)), (scr["nm"], ("lnst", 1)),
                                             (scr["s2"], ("lnst", 3)), (scr["sd"], ("lnst", 4)),
                                             (scr["rs"], ("lnst", 5))]):
                p.emit("dve", [kk], [("stt_sb", ci)],
                       lambda e, ci=ci, tile=tile: e.tensor_copy(out=stt_sb[:, ci:ci + 1], in_=tile[:, :]))
            p.dma("pool", [("stt_sb", ci) for ci in range(5)], [],
                  lambda e, tt=tt: e.dma_start(out=stto[tt * 128:(tt + 1) * 128, :], in_=stt_sb[:, :]))
        k_transpose16(p, lb, lk, lnT, "lnT", id_b, pbt, 16, dst_off=tt * 16)
        if A1_DEBUG:
            for kt in range(16):
                p.dma("pool", [("lnT", tt * 16 + kt)], [],
                      lambda e, tt=tt, kt=kt: e.dma_start(out=lnTo[:, (tt * 16 + kt) * 128:(tt * 16 + kt + 1) * 128],
                                                          in_=lnT[:, (tt * 16 + kt) * 128:(tt * 16 + kt + 1) * 128]))

    for wi, wname in enumerate(["Wq", "Wk", "Wv"]):
        bname = "b" + wname[1]
        oname = wname[1] + "o"
        k_load_wcast(p, "pool", w_sb, "w_sb", Ws[wname], 16, E, ndesc=4096)
        p.dma("pool", [], ["b_b"], lambda e, bname=bname: e.dma_start(out=b_b[:, :], in_=Bs[bname].rearrange("(o e) -> o e", o=1)))
        for tt in range(NT):
            rope = wname != "Wv"
            if rope:
                ck = "cos_t"
                sk = "sin_t"
                p.dma("sp", [], [ck], lambda e, tt=tt: e.dma_start(out=cos_t[0][:, :], in_=cosr[tt * 128:(tt + 1) * 128, :]))
                p.dma("sp", [], [sk], lambda e, tt=tt: e.dma_start(out=sin_t[0][:, :], in_=sinr[tt * 128:(tt + 1) * 128, :]))
            for nch in range(4):
                ps, psk = pmm[nch % 2]
                p.emit("pe", ["b_b", "on1"], [psk],
                       lambda e, ps=ps, nch=nch: e.matmul(ps[:, :512], on1[0:1, :128], b_b[0:1, nch * 512:(nch + 1) * 512],
                                                          start=True, stop=False))
                for kt in range(16):
                    p.emit("pe", [("lnT", tt * 16 + kt), ("w_sb", kt)], [psk],
                           lambda e, ps=ps, tt=tt, kt=kt, nch=nch: e.matmul(
                               ps[:, :512], lnT[:, (tt * 16 + kt) * 128:(tt * 16 + kt + 1) * 128],
                               w_sb[:, kt * E + nch * 512:kt * E + (nch + 1) * 512],
                               start=False, stop=(kt == 15)))
                if rope:
                    p.emit("act", [psk], [("out_f", nch)],
                           lambda e, ps=ps, nch=nch: e.activation(out=out_f[:, nch * 512:(nch + 1) * 512],
                                                                  in_=ps[:, :512], func=AF.Copy, bias=0.0, scale=1.0))
                else:
                    p.emit("act", [psk], [("out_b", tt % 2, nch)],
                           lambda e, ps=ps, nch=nch, tt=tt: e.activation(out=out_b[tt % 2][:, nch * 512:(nch + 1) * 512],
                                                                         in_=ps[:, :512], func=AF.Copy, bias=0.0, scale=1.0))
            ob = out_b[tt % 2]
            obks = [("out_b", tt % 2, n) for n in range(4)]
            if rope:
                def vw(t2, i):
                    return t2[:, :].rearrange("p (n t) -> p n t", t=2)[:, :, i:i + 1]
                qe, qo = vw(out_f, 0), vw(out_f, 1)
                ce, co = vw(cos_t[0], 0), vw(cos_t[0], 1)
                se, so = vw(sin_t[0], 0), vw(sin_t[0], 1)
                r1v = r1[:, :].rearrange("p (n t) -> p n t", t=1)
                r2v = r2[:, :].rearrange("p (n t) -> p n t", t=1)
                ofk = [("out_f", n) for n in range(4)]
                p.emit("dve", ofk + [ck], ["r1"], lambda e, qe=qe, ce=ce: e.tensor_tensor(out=r1v, in0=qe, in1=ce, op=ALU.mult))
                p.emit("dve", ofk + [sk], ["r2"], lambda e, qo=qo, se=se: e.tensor_tensor(out=r2v, in0=qo, in1=se, op=ALU.mult))
                p.emit("dve", ["r1", "r2"], [obks[0]],
                       lambda e, ob=ob: e.tensor_tensor(out=vw(ob, 0), in0=r1v, in1=r2v, op=ALU.subtract))
                p.emit("dve", ofk + [ck], ["r1"], lambda e, qo=qo, co=co: e.tensor_tensor(out=r1v, in0=qo, in1=co, op=ALU.mult))
                p.emit("dve", ofk + [sk], ["r2"], lambda e, qe=qe, so=so: e.tensor_tensor(out=r2v, in0=qe, in1=so, op=ALU.mult))
                p.emit("dve", ["r1", "r2"], [obks[1]],
                       lambda e, ob=ob: e.tensor_tensor(out=vw(ob, 1), in0=r1v, in1=r2v, op=ALU.add))
                obks = obks[:2]
            p.dma("pool", obks, [],
                  lambda e, ob=ob, tt=tt, oname=oname: e.dma_start(out=outs[oname][tt * 128:(tt + 1) * 128, :], in_=ob[:, :]))



def build_a2(nc, p, stack):
    sb = stack.enter_context
    NP = 4
    qp = nc.dram_tensor("qp", [NP, S, 128], BF16, kind="ExternalInput").ap()
    kp = nc.dram_tensor("kp", [NP, S, 128], BF16, kind="ExternalInput").ap()
    vp = nc.dram_tensor("vp", [NP, S, 128], BF16, kind="ExternalInput").ap()
    maskT = nc.dram_tensor("maskT", [4, 128, 512], F32, kind="ExternalInput").ap()
    onesc = nc.dram_tensor("onesc", [128, 1], BF16, kind="ExternalInput").ap()
    ones1 = nc.dram_tensor("ones1", [1, 128], BF16, kind="ExternalInput").ap()
    op = nc.dram_tensor("op", [NP, S, 128], BF16, kind="ExternalOutput").ap()

    qT = sb(nc.sbuf_tensor("qT", [128, S], BF16))
    kT = sb(nc.sbuf_tensor("kT", [128, S], BF16))
    v_sb = sb(nc.sbuf_tensor("v_sb", [128, S], BF16))
    pT = [sb(nc.sbuf_tensor(f"pT{i}", [128, 512], BF16)) for i in range(2)]
    dg = sb(nc.sbuf_tensor("dg", [128, 512], F32))
    mk_sb = sb(nc.sbuf_tensor("mk_sb", [128, 4 * 512], F32))
    onc = sb(nc.sbuf_tensor("onc", [128, 1], BF16))
    on1 = sb(nc.sbuf_tensor("on1", [1, 128], BF16))
    zb = sb(nc.sbuf_tensor("zb", [128, 1], F32))
    rinv = sb(nc.sbuf_tensor("rinv", [128, 4], F32))
    o_b = sb(nc.sbuf_tensor("o_b", [128, 4 * 128], BF16))

    rs_sb = sb(nc.sbuf_tensor("rs_sb", [1, 512], F32))
    onesf = sb(nc.sbuf_tensor("onesf", [1, 1], F32))
    pS = (sb(nc.psum_tensor("pS", [128, 512], F32)), "pS")
    pO = [(sb(nc.psum_tensor(f"pO{i}", [128, 512], F32)), f"pO{i}") for i in range(4)]
    pR = (sb(nc.psum_tensor("pR", [128, 512], F32)), "pR")
    pQ = (sb(nc.psum_tensor("pQ", [128, 512], F32)), "pQ")

    sc = 1.0 / np.sqrt(128.0)
    p.emit("dve", [], ["zb"], lambda e: e.memset(zb[:, :], 0.0))
    p.emit("dve", [], ["onesf"], lambda e: e.memset(onesf[:, :], 1.0))
    p.dma("sp", [], ["onc"], lambda e: e.dma_start(out=onc[:, :], in_=onesc[:, :]))
    p.dma("sp", [], ["on1"], lambda e: e.dma_start(out=on1[:, :], in_=ones1[:, :]))
    for m in range(4):
        p.dma("sp", [], [("mk_sb", m)],
              lambda e, m=m: e.dma_start(out=mk_sb[:, m * 512:(m + 1) * 512], in_=maskT[m, :, :]))

    for pr in range(4):
        p.dma("sp", [], ["qT"], lambda e, pr=pr: e.dma_start(out=qT[:, :], in_=qp[pr, :, :], transpose=True))
        p.dma("sp", [], ["kT"], lambda e, pr=pr: e.dma_start(out=kT[:, :], in_=kp[pr, :, :], transpose=True))
        for j in range(16):
            p.dma("pool", [], [("v_sb", j)],
                  lambda e, pr=pr, j=j: e.dma_start(
                      out=v_sb[:, :].rearrange("p (c d) -> p c d", d=128)[:, j * 4:(j + 1) * 4, :],
                      in_=vp[pr, j * 512:(j + 1) * 512, :].rearrange("(c p) d -> p c d", p=128)))
        for qs in range(16):
            nkc = 4 * qs + 4
            for kc in range(nkc):
                m = kc - 4 * qs
                ps, psk = pS
                p.emit("pe", ["qT", "kT"], [psk],
                       lambda e, ps=ps, kc=kc, qs=qs: e.matmul(ps[:, :512], kT[:, kc * 128:(kc + 1) * 128],
                                                               qT[:, qs * 512:(qs + 1) * 512], start=True, stop=True))
                pt = pT[kc % 2]
                ptk = ("pT", kc % 2)
                if m < 0:
                    p.emit("act", [psk, "zb"], [ptk],
                           lambda e, ps=ps, pt=pt: e.activation(out=pt[:, :], in_=ps[:, :512], func=AF.Exp,
                                                                bias=zb[:, :], scale=sc))
                else:
                    p.emit("dve", [psk, ("mk_sb", m)], ["dg"],
                           lambda e, ps=ps, m=m: e.scalar_tensor_tensor(out=dg[:, :], in0=ps[:, :512], scalar=sc,
                                                                        in1=mk_sb[:, m * 512:(m + 1) * 512],
                                                                        op0=ALU.mult, op1=ALU.add))
                    p.emit("act", ["dg", "zb"], [ptk],
                           lambda e, pt=pt: e.activation(out=pt[:, :], in_=dg[:, :], func=AF.Exp,
                                                         bias=zb[:, :], scale=1.0))
                first = kc == 0
                last = kc == nkc - 1
                for s in range(4):
                    po, pok = pO[s]
                    p.emit("pe", [ptk, ("v_sb", kc // 4)], [pok],
                           lambda e, po=po, pt=pt, s=s, kc=kc, first=first, last=last: e.matmul(
                               po[:, :128], pt[:, s * 128:(s + 1) * 128], v_sb[:, kc * 128:(kc + 1) * 128],
                               start=first, stop=last))
                pr_, prk = pR
                p.emit("pe", [ptk, "onc"], [prk],
                       lambda e, pr_=pr_, pt=pt, first=first, last=last: e.matmul(
                           pr_[0:1, :512], onc[:, :], pt[:, :], start=first, stop=last))
            p.emit("act", [pR[1]], ["rs_sb"],
                   lambda e: e.activation(out=rs_sb[:, :], in_=pR[0][0:1, :512], func=AF.Copy, bias=0.0, scale=1.0))
            pq, pqk = pQ
            for s in range(4):
                p.emit("pe", ["rs_sb", "onesf"], [pqk],
                       lambda e, pq=pq, s=s: e.matmul(pq[:, s:s + 1], rs_sb[0:1, s * 128:(s + 1) * 128],
                                                      onesf[0:1, 0:1], start=True, stop=True, skip_group_check=True))
            p.emit("dve", [pqk], ["rinv"], lambda e, pq=pq: e.reciprocal(out=rinv[:, :], in_=pq[:, 0:4]))
            for s in range(4):
                p.emit("dve", [pO[s][1], "rinv"], [("o_b", s)],
                       lambda e, s=s: e.tensor_scalar(out=o_b[:, s * 128:(s + 1) * 128], in0=pO[s][0][:, :128],
                                                      scalar1=rinv[:, s:s + 1], scalar2=None, op0=ALU.mult))
                p.dma("pool", [("o_b", s)], [],
                      lambda e, pr=pr, qs=qs, s=s: e.dma_start(out=op[pr, qs * 512 + s * 128:qs * 512 + (s + 1) * 128, :],
                                                               in_=o_b[:, s * 128:(s + 1) * 128]))




def build_a3(nc, p, stack):
    sb = stack.enter_context
    x = nc.dram_tensor("x", [NTOK, E], F32, kind="ExternalInput").ap()
    at = nc.dram_tensor("at", [NTOK, E], BF16, kind="ExternalInput").ap()
    Wo = nc.dram_tensor("Wo", [E, E], F32, kind="ExternalInput").ap()
    bo = nc.dram_tensor("bo", [E], F32, kind="ExternalInput").ap()
    gwT = nc.dram_tensor("gwT", [E, 8], F32, kind="ExternalInput").ap()
    ident = nc.dram_tensor("ident", [128, 128], BF16, kind="ExternalInput").ap()
    ones1 = nc.dram_tensor("ones1", [1, 128], BF16, kind="ExternalInput").ap()
    x1 = nc.dram_tensor("x1", [NTOK, E], F32, kind="ExternalOutput").ap()
    lnTo = nc.dram_tensor("lnTo", [16, 128, NTOK], BF16, kind="ExternalOutput").ap()
    cwo = nc.dram_tensor("cwo", [NTOK, 8], F32, kind="ExternalOutput").ap()

    x_t = [sb(nc.sbuf_tensor(f"x_t{i}", [128, E], F32)) for i in range(2)]
    at_t = [sb(nc.sbuf_tensor(f"at_t{i}", [128, E], BF16)) for i in range(2)]
    atT = sb(nc.sbuf_tensor("atT", [128, 16 * 128], BF16))
    w_sb = sb(nc.sbuf_tensor("w_sb", [128, 16 * E], BF16))
    gw_sb = sb(nc.sbuf_tensor("gw_sb", [128, 16 * 8], BF16))
    b_f = sb(nc.sbuf_tensor("b_f", [1, E], F32))
    b_b = sb(nc.sbuf_tensor("b_b", [1, E], BF16))
    on1 = sb(nc.sbuf_tensor("on1", [1, 128], BF16))
    id_b = sb(nc.sbuf_tensor("id_b", [128, 128], BF16))
    x1_sb = [sb(nc.sbuf_tensor(f"x1_sb{i}", [128, E], F32)) for i in range(2)]
    ln_b = [sb(nc.sbuf_tensor(f"ln_b{i}", [128, E], BF16)) for i in range(2)]
    lnT_t = sb(nc.sbuf_tensor("lnT_t", [128, 16 * 128], BF16))
    s8 = sb(nc.sbuf_tensor("s8", [128, 8], F32))
    m1 = sb(nc.sbuf_tensor("m1", [128, 8], F32))
    s2_ = sb(nc.sbuf_tensor("s2_", [128, 8], F32))
    sel = sb(nc.sbuf_tensor("sel", [128, 8], F32))
    w8 = sb(nc.sbuf_tensor("w8", [128, 8], F32))
    cw_sb = sb(nc.sbuf_tensor("cw_sb", [128, 8], F32))
    ngm = sb(nc.sbuf_tensor("ngm", [128, 1], F32))
    rsu = sb(nc.sbuf_tensor("rsu", [128, 1], F32))
    riv = sb(nc.sbuf_tensor("riv", [128, 1], F32))
    scr = {k: sb(nc.sbuf_tensor("scr_" + k, [128, d], F32)) for k, d in
           [("sq", E), ("s1", 1), ("s2", 1), ("nm", 1), ("sd", 1), ("rs", 1), ("eps", 1)]}
    pmm = [(sb(nc.psum_tensor(f"pmm{i}", [128, 512], F32)), f"pmm{i}") for i in range(2)]
    pbt = [(sb(nc.psum_tensor(f"pbt{i}", [128, 1024], BF16)), f"pbt{i}") for i in range(2)]
    pg = (sb(nc.psum_tensor("pg", [128, 512], F32)), "pg")

    p.emit("dve", [], ["eps"], lambda e: e.memset(scr["eps"][:, :], 1e-5))
    p.dma("sp", [], ["ident"], lambda e: e.dma_start(out=id_b[:, :], in_=ident[:, :]))
    p.dma("sp", [], ["on1"], lambda e: e.dma_start(out=on1[:, :], in_=ones1[:, :]))
    k_load_wcast(p, "pool", w_sb, "w_sb", Wo, 16, E, ndesc=4096)
    p.dma("pool", [], [("gw_sb", 0)],
          lambda e: e.dma_start(out=gw_sb[:, :].rearrange("p (kt n) -> p kt n", n=8),
                                in_=gwT.rearrange("(kt p) n -> p kt n", p=128)))
    p.dma("sp", [], ["b_f"], lambda e: e.dma_start(out=b_f[:, :], in_=bo.rearrange("(o e) -> o e", o=1)))
    p.emit("dve", ["b_f"], ["b_b"], lambda e: e.tensor_copy(out=b_b[:, :], in_=b_f[:, :]))

    for tt in range(NT):
        xk, xb = ("x_t", tt % 2), x_t[tt % 2]
        ak, ab = ("at_t", tt % 2), at_t[tt % 2]
        p.dma("sp", [], [xk], lambda e, tt=tt, xb=xb: e.dma_start(out=xb[:, :], in_=x[tt * 128:(tt + 1) * 128, :]))
        p.dma("sp", [], [ak], lambda e, tt=tt, ab=ab: e.dma_start(out=ab[:, :], in_=at[tt * 128:(tt + 1) * 128, :]))
        k_transpose16(p, ab, ak, atT, "atT", id_b, pbt, 16)
        x1b = x1_sb[tt % 2]
        x1k = ("x1_sb", tt % 2)
        for nch in range(4):
            ps, psk = pmm[nch % 2]
            p.emit("pe", ["b_b", "on1"], [psk],
                   lambda e, ps=ps, nch=nch: e.matmul(ps[:, :512], on1[0:1, :128], b_b[0:1, nch * 512:(nch + 1) * 512],
                                                      start=True, stop=False))
            for kt in range(16):
                p.emit("pe", [("atT", kt), ("w_sb", kt)], [psk],
                       lambda e, ps=ps, kt=kt, nch=nch: e.matmul(ps[:, :512], atT[:, kt * 128:(kt + 1) * 128],
                                                                 w_sb[:, kt * E + nch * 512:kt * E + (nch + 1) * 512],
                                                                 start=False, stop=(kt == 15)))
            p.emit("dve", [psk, xk], [(x1k, nch)],
                   lambda e, ps=ps, nch=nch, xb=xb, x1b=x1b: e.tensor_tensor(
                       out=x1b[:, nch * 512:(nch + 1) * 512], in0=ps[:, :512],
                       in1=xb[:, nch * 512:(nch + 1) * 512], op=ALU.add))
        x1ks = [(x1k, n) for n in range(4)]
        p.dma("pool", x1ks, [], lambda e, tt=tt, x1b=x1b: e.dma_start(out=x1[tt * 128:(tt + 1) * 128, :], in_=x1b[:, :]))
        lb, lk = ln_b[tt % 2], ("ln_b", tt % 2)
        k_ln(p, x1ks[0], x1b[:, :], lb[:, :], lk, scr, E, extra_reads=x1ks[1:])
        k_transpose16(p, lb, lk, lnT_t, "lnT_t", id_b, pbt, 16)
        for kt in range(16):
            p.dma("pool", [("lnT_t", kt)], [],
                  lambda e, tt=tt, kt=kt: e.dma_start(out=lnTo[kt, :, tt * 128:(tt + 1) * 128],
                                                      in_=lnT_t[:, kt * 128:(kt + 1) * 128]))
        ps, psk = pg
        for kt in range(16):
            p.emit("pe", [("lnT_t", kt), ("gw_sb", 0)], [psk],
                   lambda e, ps=ps, kt=kt: e.matmul(ps[:, :8], lnT_t[:, kt * 128:(kt + 1) * 128],
                                                    gw_sb[:, kt * 8:(kt + 1) * 8], start=(kt == 0), stop=(kt == 15)))
        p.emit("dve", [psk], ["ngm"], lambda e, ps=ps: e.reduce_max(ngm[:, :], ps[:, 0:8], axis=AX.X, negate=True))
        p.emit("act", [psk, "ngm"], ["s8"],
               lambda e, ps=ps: e.activation(out=s8[:, :], in_=ps[:, 0:8], func=AF.Exp, bias=ngm[:, :], scale=1.0))
        p.emit("dve", ["s8"], ["m1"],
               lambda e: e.tensor_scalar(out=m1[:, :], in0=s8[:, :], scalar1=1.0, scalar2=None, op0=ALU.is_ge))
        p.emit("dve", ["s8", "m1"], ["s2_"],
               lambda e: e.scalar_tensor_tensor(out=s2_[:, :], in0=m1[:, :], scalar=-2.0, in1=s8[:, :],
                                                op0=ALU.mult, op1=ALU.add))
        p.emit("dve", ["s2_", "s8"], ["ngm"], lambda e: e.reduce_max(ngm[:, :], s2_[:, :], axis=AX.X, negate=True))
        p.emit("dve", ["s8", "ngm"], ["sel"],
               lambda e: e.tensor_scalar(out=sel[:, :], in0=s8[:, :], scalar1=ngm[:, :], scalar2=0.0,
                                         op0=ALU.add, op1=ALU.is_ge))
        p.emit("dve", ["s8", "sel"], ["w8"],
               lambda e: e.tensor_tensor(out=w8[:, :], in0=s8[:, :], in1=sel[:, :], op=ALU.mult))
        p.emit("dve", ["w8"], ["rsu"], lambda e: e.reduce_sum(rsu[:, :], w8[:, :], axis=AX.X))
        p.emit("dve", ["rsu"], ["rsu"],
               lambda e: e.tensor_scalar(out=rsu[:, :], in0=rsu[:, :], scalar1=1e-20, scalar2=None, op0=ALU.add))
        p.emit("dve", ["rsu"], ["riv"], lambda e: e.reciprocal(out=riv[:, :], in_=rsu[:, :]))
        p.emit("dve", ["w8", "riv"], ["cw_sb"],
               lambda e: e.tensor_scalar(out=cw_sb[:, :], in0=w8[:, :], scalar1=riv[:, :], scalar2=None, op0=ALU.mult))
        p.dma("pool", ["cw_sb"], [],
              lambda e, tt=tt: e.dma_start(out=cwo[tt * 128:(tt + 1) * 128, :], in_=cw_sb[:, :]))



def build_b1(nc, p, stack):
    sb = stack.enter_context
    I = 1408
    w1 = nc.dram_tensor("w1", [E, 2 * I], F32, kind="ExternalInput").ap()
    w2 = nc.dram_tensor("w2", [I, E], F32, kind="ExternalInput").ap()
    lnTd = nc.dram_tensor("lnTd", [16, 128, T], BF16, kind="ExternalInput").ap()
    cwc = nc.dram_tensor("cwc", [T], F32, kind="ExternalInput").ap()
    yo = nc.dram_tensor("yo", [T, E], BF16, kind="ExternalOutput").ap()

    w1_sb = sb(nc.sbuf_tensor("w1_sb", [128, 16 * 2 * I], BF16))
    w2_sb = sb(nc.sbuf_tensor("w2_sb", [128, 11 * E], BF16))
    lnT_g = [sb(nc.sbuf_tensor(f"lnT_g{i}", [128, 16 * 512], BF16)) for i in range(2)]
    cw_sb = [sb(nc.sbuf_tensor(f"cw_sb{i}", [128, 4], F32)) for i in range(2)]
    g_tmp = sb(nc.sbuf_tensor("g_tmp", [128, 512], BF16))
    fuseT = sb(nc.sbuf_tensor("fuseT", [128, 11 * 512], BF16))
    y_sb = [sb(nc.sbuf_tensor(f"y_sb{i}", [128, E], BF16)) for i in range(2)]
    zb = sb(nc.sbuf_tensor("zb", [128, 1], F32))
    pA = [(sb(nc.psum_tensor(f"pA{i}", [128, 512], F32)), f"pA{i}") for i in range(4)]

    p.emit("dve", [], ["zb"], lambda e: e.memset(zb[:, :], 0.0))
    k_load_wcast(p, "pool", w1_sb, "w1_sb", w1, 16, 2 * I, ndesc=5632)
    k_load_wcast(p, "pool", w2_sb, "w2_sb", w2, 11, E, ndesc=4096)

    NG = T // 512
    for g in range(NG):
        gb = g % 2
        lg, cwb = lnT_g[gb], cw_sb[gb]
        for kt in range(16):
            p.dma("sp", [], [("lnT_g", gb, kt)],
                  lambda e, g=g, kt=kt, lg=lg: e.dma_start(out=lg[:, kt * 512:(kt + 1) * 512],
                                                           in_=lnTd[kt, :, g * 512:(g + 1) * 512]))
        for s in range(4):
            p.dma("sp", [], [("cw_sb", gb, s)],
                  lambda e, g=g, s=s, cwb=cwb: e.dma_start(out=cwb[:, s:s + 1],
                                                           in_=cwc[g * 512 + s * 128:g * 512 + (s + 1) * 128].rearrange("(p o) -> p o", o=1)))
        lgks = [("lnT_g", gb, kt) for kt in range(16)]
        for it in range(11):
            psg, psgk = pA[(2 * it) % 4]
            psu, psuk = pA[(2 * it + 1) % 4]
            for kt in range(16):
                p.emit("pe", [("w1_sb", kt), lgks[kt]], [psgk],
                       lambda e, psg=psg, kt=kt, it=it, lg=lg: e.matmul(
                           psg[:, :512], w1_sb[:, kt * 2 * I + it * 128:kt * 2 * I + (it + 1) * 128],
                           lg[:, kt * 512:(kt + 1) * 512], start=(kt == 0), stop=(kt == 15)))
            p.emit("act", [psgk, "zb"], ["g_tmp"],
                   lambda e, psg=psg: e.activation(out=g_tmp[:, :], in_=psg[:, :512], func=AF.Silu,
                                                   bias=zb[:, :], scale=1.0))
            for kt in range(16):
                p.emit("pe", [("w1_sb", kt), lgks[kt]], [psuk],
                       lambda e, psu=psu, kt=kt, it=it, lg=lg: e.matmul(
                           psu[:, :512], w1_sb[:, kt * 2 * I + I + it * 128:kt * 2 * I + I + (it + 1) * 128],
                           lg[:, kt * 512:(kt + 1) * 512], start=(kt == 0), stop=(kt == 15)))
            p.emit("dve", [psuk, "g_tmp"], [("fuseT", it)],
                   lambda e, psu=psu, it=it: e.tensor_tensor(out=fuseT[:, it * 512:(it + 1) * 512],
                                                             in0=psu[:, :512], in1=g_tmp[:, :], op=ALU.mult))
        yb = y_sb[gb]
        for s in range(4):
            for nch in range(4):
                ps, psk = pA[nch % 4]
                for it in range(11):
                    p.emit("pe", [("fuseT", it), ("w2_sb", it)], [psk],
                           lambda e, ps=ps, it=it, s=s, nch=nch: e.matmul(
                               ps[:, :512], fuseT[:, it * 512 + s * 128:it * 512 + (s + 1) * 128],
                               w2_sb[:, it * E + nch * 512:it * E + (nch + 1) * 512],
                               start=(it == 0), stop=(it == 10)))
                p.emit("dve", [psk, ("cw_sb", gb, s)], [("y_sb", gb, nch)],
                       lambda e, ps=ps, nch=nch, s=s, yb=yb, cwb=cwb: e.tensor_scalar(
                           out=yb[:, nch * 512:(nch + 1) * 512], in0=ps[:, :512],
                           scalar1=cwb[:, s:s + 1], scalar2=None, op0=ALU.mult))
            p.dma("pool", [("y_sb", gb, n) for n in range(4)], [],
                  lambda e, g=g, s=s, yb=yb: e.dma_start(out=yo[g * 512 + s * 128:g * 512 + (s + 1) * 128, :],
                                                         in_=yb[:, :]))



def build_b2(nc, p, stack):
    sb = stack.enter_context
    I = 1408
    x1 = nc.dram_tensor("x1", [NTOK, E], F32, kind="ExternalInput").ap()
    lnTd = nc.dram_tensor("lnTd", [16, 128, NTOK], BF16, kind="ExternalInput").ap()
    ye = nc.dram_tensor("ye", [8, NTOK, E], BF16, kind="ExternalInput").ap()
    sw1T = nc.dram_tensor("sw1T", [E, I], F32, kind="ExternalInput").ap()
    sw2T = nc.dram_tensor("sw2T", [E, I], F32, kind="ExternalInput").ap()
    sw3T = nc.dram_tensor("sw3T", [I, E], F32, kind="ExternalInput").ap()
    ident = nc.dram_tensor("ident", [128, 128], BF16, kind="ExternalInput").ap()
    x2 = nc.dram_tensor("x2", [NTOK, E], F32, kind="ExternalOutput").ap()

    w1_sb = sb(nc.sbuf_tensor("w1_sb", [128, 16 * I], BF16))
    w2_sb = sb(nc.sbuf_tensor("w2_sb", [128, 16 * I], BF16))
    w3_sb = sb(nc.sbuf_tensor("w3_sb", [128, 11 * E], BF16))
    lnT_g = sb(nc.sbuf_tensor("lnT_g", [128, 16 * 512], BF16))
    g_tmp = sb(nc.sbuf_tensor("g_tmp", [128, 512], BF16))
    fuseT = sb(nc.sbuf_tensor("fuseT", [128, 11 * 512], BF16))
    x1_sb = sb(nc.sbuf_tensor("x1_sb", [128, E], F32))
    ye_sb = [sb(nc.sbuf_tensor(f"ye_sb{i}", [128, 8 * 512], BF16)) for i in range(2)]
    t1_sb = sb(nc.sbuf_tensor("t1_sb", [128, E], F32))
    id_b = sb(nc.sbuf_tensor("id_b", [128, 128], BF16))
    zb = sb(nc.sbuf_tensor("zb", [128, 1], F32))
    pA = [(sb(nc.psum_tensor(f"pA{i}", [128, 512], F32)), f"pA{i}") for i in range(4)]

    p.emit("dve", [], ["zb"], lambda e: e.memset(zb[:, :], 0.0))
    p.dma("sp", [], ["ident"], lambda e: e.dma_start(out=id_b[:, :], in_=ident[:, :]))
    k_load_wcast(p, "pool", w1_sb, "w1_sb", sw1T, 16, I, ndesc=2944)
    k_load_wcast(p, "pool", w2_sb, "w2_sb", sw2T, 16, I, ndesc=2944)
    k_load_wcast(p, "pool", w3_sb, "w3_sb", sw3T, 11, E, ndesc=4096)

    for g in range(NTOK // 512):
        for kt in range(16):
            p.dma("sp", [], [("lnT_g", kt)],
                  lambda e, g=g, kt=kt: e.dma_start(out=lnT_g[:, kt * 512:(kt + 1) * 512],
                                                    in_=lnTd[kt, :, g * 512:(g + 1) * 512]))
        for it in range(11):
            psg, psgk = pA[(2 * it) % 4]
            psu, psuk = pA[(2 * it + 1) % 4]
            for kt in range(16):
                p.emit("pe", [("w1_sb", kt), ("lnT_g", kt)], [psgk],
                       lambda e, psg=psg, kt=kt, it=it: e.matmul(
                           psg[:, :512], w1_sb[:, kt * I + it * 128:kt * I + (it + 1) * 128],
                           lnT_g[:, kt * 512:(kt + 1) * 512], start=(kt == 0), stop=(kt == 15)))
            p.emit("act", [psgk, "zb"], ["g_tmp"],
                   lambda e, psg=psg: e.activation(out=g_tmp[:, :], in_=psg[:, :512], func=AF.Silu,
                                                   bias=zb[:, :], scale=1.0))
            for kt in range(16):
                p.emit("pe", [("w2_sb", kt), ("lnT_g", kt)], [psuk],
                       lambda e, psu=psu, kt=kt, it=it: e.matmul(
                           psu[:, :512], w2_sb[:, kt * I + it * 128:kt * I + (it + 1) * 128],
                           lnT_g[:, kt * 512:(kt + 1) * 512], start=(kt == 0), stop=(kt == 15)))
            p.emit("dve", [psuk, "g_tmp"], [("fuseT", it)],
                   lambda e, psu=psu, it=it: e.tensor_tensor(out=fuseT[:, it * 512:(it + 1) * 512],
                                                             in0=psu[:, :512], in1=g_tmp[:, :], op=ALU.mult))
        for s in range(4):
            row0 = g * 512 + s * 128
            p.dma("sp", [], ["x1_sb"], lambda e, row0=row0: e.dma_start(out=x1_sb[:, :], in_=x1[row0:row0 + 128, :]))
            for nch in range(4):
                yb = ye_sb[nch % 2]
                ybp = nch % 2
                for ex in range(8):
                    p.dma("pool", [], [("ye_sb", ybp, ex)],
                          lambda e, ex=ex, row0=row0, nch=nch, yb=yb: e.dma_start(
                              out=yb[:, ex * 512:(ex + 1) * 512],
                              in_=ye[ex, row0:row0 + 128, nch * 512:(nch + 1) * 512]))
                ps, psk = pA[nch % 4]
                for it in range(11):
                    p.emit("pe", [("fuseT", it), ("w3_sb", it)], [psk],
                           lambda e, ps=ps, it=it, s=s, nch=nch: e.matmul(
                               ps[:, :512], fuseT[:, it * 512 + s * 128:it * 512 + (s + 1) * 128],
                               w3_sb[:, it * E + nch * 512:it * E + (nch + 1) * 512],
                               start=(it == 0), stop=False))
                for ex in range(8):
                    p.emit("pe", [("ye_sb", ybp, ex), "ident"], [psk],
                           lambda e, ps=ps, ex=ex, yb=yb: e.matmul(
                               ps[:, :512], id_b[:, :], yb[:, ex * 512:(ex + 1) * 512],
                               start=False, stop=(ex == 7)))
                p.emit("dve", [psk, "x1_sb"], [("t1_sb", nch)],
                       lambda e, ps=ps, nch=nch: e.tensor_tensor(out=t1_sb[:, nch * 512:(nch + 1) * 512],
                                                                 in0=ps[:, :512],
                                                                 in1=x1_sb[:, nch * 512:(nch + 1) * 512], op=ALU.add))
            p.dma("pool", [("t1_sb", n) for n in range(4)], [],
                  lambda e, row0=row0: e.dma_start(out=x2[row0:row0 + 128, :], in_=t1_sb[:, :]))



def build_c1(nc, p, stack):
    sb = stack.enter_context
    O = 6144
    x = nc.dram_tensor("x", [NTOK, E], F32, kind="ExternalInput").ap()
    wT = nc.dram_tensor("wT", [E, O], F32, kind="ExternalInput").ap()
    ident = nc.dram_tensor("ident", [128, 128], BF16, kind="ExternalInput").ap()
    qkv = nc.dram_tensor("qkv", [NTOK, O], BF16, kind="ExternalOutput").ap()
    lnTo = nc.dram_tensor("lnTo", [16, 128, NTOK], BF16, kind="ExternalOutput").ap()

    x_t = [sb(nc.sbuf_tensor(f"x_t{i}", [128, E], F32)) for i in range(2)]
    ln_b = [sb(nc.sbuf_tensor(f"ln_b{i}", [128, E], BF16)) for i in range(2)]
    lnT = sb(nc.sbuf_tensor("lnT", [128, NT * 16 * 128], BF16))
    w_c = [sb(nc.sbuf_tensor(f"w_c{i}", [128, 16 * 512], BF16)) for i in range(2)]
    id_b = sb(nc.sbuf_tensor("id_b", [128, 128], BF16))
    o_b = sb(nc.sbuf_tensor("o_b", [128, 512], BF16))
    zb = sb(nc.sbuf_tensor("zb", [128, 1], F32))
    scr = {k: sb(nc.sbuf_tensor("scr_" + k, [128, d], F32)) for k, d in
           [("sq", E), ("s1", 1), ("s2", 1), ("nm", 1), ("sd", 1), ("rs", 1), ("eps", 1)]}
    pmm = [(sb(nc.psum_tensor(f"pmm{i}", [128, 512], F32)), f"pmm{i}") for i in range(2)]
    pbt = [(sb(nc.psum_tensor(f"pbt{i}", [128, 1024], BF16)), f"pbt{i}") for i in range(2)]

    p.emit("dve", [], ["eps"], lambda e: e.memset(scr["eps"][:, :], 1e-5))
    p.emit("dve", [], ["zb"], lambda e: e.memset(zb[:, :], 0.0))
    p.dma("sp", [], ["ident"], lambda e: e.dma_start(out=id_b[:, :], in_=ident[:, :]))

    for tt in range(NT):
        xk, xb = ("x_t", tt % 2), x_t[tt % 2]
        p.dma("sp", [], [xk], lambda e, tt=tt, xb=xb: e.dma_start(out=xb[:, :], in_=x[tt * 128:(tt + 1) * 128, :]))
        lb, lk = ln_b[tt % 2], ("ln_b", tt % 2)
        k_ln(p, xk, xb[:, :], lb[:, :], lk, scr, E)
        if A1_DEBUG:
            p.dma("pool", [lk], [], lambda e, tt=tt, lb=lb: e.dma_start(out=lno[tt * 128:(tt + 1) * 128, :], in_=lb[:, :]))
            for ci, (tile, kk) in enumerate([(scr["s1"], ("lnst", 0)), (scr["nm"], ("lnst", 1)),
                                             (scr["s2"], ("lnst", 3)), (scr["sd"], ("lnst", 4)),
                                             (scr["rs"], ("lnst", 5))]):
                p.emit("dve", [kk], [("stt_sb", ci)],
                       lambda e, ci=ci, tile=tile: e.tensor_copy(out=stt_sb[:, ci:ci + 1], in_=tile[:, :]))
            p.dma("pool", [("stt_sb", ci) for ci in range(5)], [],
                  lambda e, tt=tt: e.dma_start(out=stto[tt * 128:(tt + 1) * 128, :], in_=stt_sb[:, :]))
        k_transpose16(p, lb, lk, lnT, "lnT", id_b, pbt, 16, dst_off=tt * 16)
        if A1_DEBUG:
            for kt in range(16):
                p.dma("pool", [("lnT", tt * 16 + kt)], [],
                      lambda e, tt=tt, kt=kt: e.dma_start(out=lnTo[:, (tt * 16 + kt) * 128:(tt * 16 + kt + 1) * 128],
                                                          in_=lnT[:, (tt * 16 + kt) * 128:(tt * 16 + kt + 1) * 128]))
        for kt in range(16):
            p.dma("pool", [("lnT", tt * 16 + kt)], [],
                  lambda e, tt=tt, kt=kt: e.dma_start(out=lnTo[kt, :, tt * 128:(tt + 1) * 128],
                                                      in_=lnT[:, (tt * 16 + kt) * 128:(tt * 16 + kt + 1) * 128]))
    for nch in range(12):
        wb, wbk = w_c[nch % 2], nch % 2
        for kt in range(16):
            p.dma("pool", [], [("w_c", wbk, kt)],
                  lambda e, nch=nch, kt=kt, wb=wb: e.dma_start(out=wb[:, kt * 512:(kt + 1) * 512],
                                                               in_=wT[kt * 128:(kt + 1) * 128, nch * 512:(nch + 1) * 512]))
        for tt in range(NT):
            ps, psk = pmm[tt % 2]
            for kt in range(16):
                p.emit("pe", [("lnT", tt * 16 + kt), ("w_c", wbk, kt)], [psk],
                       lambda e, ps=ps, tt=tt, kt=kt, wb=wb: e.matmul(
                           ps[:, :512], lnT[:, (tt * 16 + kt) * 128:(tt * 16 + kt + 1) * 128],
                           wb[:, kt * 512:(kt + 1) * 512], start=(kt == 0), stop=(kt == 15)))
            p.emit("act", [psk, "zb"], ["o_b"],
                   lambda e, ps=ps: e.activation(out=o_b[:, :], in_=ps[:, :512], func=AF.Silu, bias=zb[:, :], scale=1.0))
            p.dma("pool", ["o_b"], [],
                  lambda e, tt=tt, nch=nch: e.dma_start(out=qkv[tt * 128:(tt + 1) * 128, nch * 512:(nch + 1) * 512],
                                                        in_=o_b[:, :]))



def build_c2(nc, p, stack):
    sb = stack.enter_context
    NP = 8
    qp = nc.dram_tensor("qp", [NP, S, 64], BF16, kind="ExternalInput").ap()
    kp = nc.dram_tensor("kp", [NP, S, 64], BF16, kind="ExternalInput").ap()
    vp = nc.dram_tensor("vp", [NP, S, 64], BF16, kind="ExternalInput").ap()
    ddT = nc.dram_tensor("ddT", [NP, 2, 128, 256], F32, kind="ExternalInput").ap()
    qd = nc.dram_tensor("qd", [NP, 128, 2], F32, kind="ExternalInput").ap()
    kd = nc.dram_tensor("kd", [NP, 128, 2], F32, kind="ExternalInput").ap()
    bdc = nc.dram_tensor("bdc", [NP, 64], F32, kind="ExternalInput").ap()
    ident = nc.dram_tensor("ident", [128, 128], BF16, kind="ExternalInput").ap()
    op = nc.dram_tensor("op", [NP, S, 64], BF16, kind="ExternalOutput").ap()

    q_sb = sb(nc.sbuf_tensor("q_sb", [128, 128], BF16))
    k_sb = sb(nc.sbuf_tensor("k_sb", [128, 128], BF16))
    v_sb = sb(nc.sbuf_tensor("v_sb", [128, 128], BF16))
    kdk = sb(nc.sbuf_tensor("kdk", [128, 128], BF16))
    qT_sb = sb(nc.sbuf_tensor("qT_sb", [64, 256], BF16))
    kT_sb = sb(nc.sbuf_tensor("kT_sb", [64, 256], BF16))
    awT = sb(nc.sbuf_tensor("awT", [128, 2 * 256], BF16))
    dd_sb = sb(nc.sbuf_tensor("dd_sb", [128, 2 * 256], F32))
    qd_sb = sb(nc.sbuf_tensor("qd_sb", [128, 2], F32))
    kd_sb = sb(nc.sbuf_tensor("kd_sb", [128, 2], F32))
    bd_sb = sb(nc.sbuf_tensor("bd_sb", [64, 1], F32))
    st_f = sb(nc.sbuf_tensor("st_f", [64, 64], F32))
    st_b = sb(nc.sbuf_tensor("st_b", [64, 64], BF16))
    y_f = sb(nc.sbuf_tensor("y_f", [128, 2 * 64], F32))
    y_o = sb(nc.sbuf_tensor("y_o", [128, 2 * 64], BF16))
    id_b = sb(nc.sbuf_tensor("id_b", [128, 128], BF16))

    pT = [(sb(nc.psum_tensor(f"pT{i}", [128, 1024], BF16)), f"pT{i}") for i in range(2)]
    pA = (sb(nc.psum_tensor("pA", [128, 512], F32)), "pA")
    pN = (sb(nc.psum_tensor("pN", [128, 512], F32)), "pN")
    pI = [(sb(nc.psum_tensor(f"pI{i}", [128, 512], F32)), f"pI{i}") for i in range(2)]
    pU = (sb(nc.psum_tensor("pU", [128, 512], F32)), "pU")

    p.dma("sp", [], ["ident"], lambda e: e.dma_start(out=id_b[:, :], in_=ident[:, :]))

    for pr in range(NP):
        p.dma("sp", [], ["dd0"], lambda e, pr=pr: e.dma_start(out=dd_sb[:, 0:256], in_=ddT[pr, 0, :, :]))
        p.dma("sp", [], ["dd1"], lambda e, pr=pr: e.dma_start(out=dd_sb[:, 256:512], in_=ddT[pr, 1, :, :]))
        p.dma("sp", [], ["qd_sb"], lambda e, pr=pr: e.dma_start(out=qd_sb[:, :], in_=qd[pr, :, :]))
        p.dma("sp", [], ["kd_sb"], lambda e, pr=pr: e.dma_start(out=kd_sb[:, :], in_=kd[pr, :, :]))
        p.dma("sp", [], ["bd_sb"], lambda e, pr=pr: e.dma_start(out=bd_sb[:, :], in_=bdc[pr, :].rearrange("(p o) -> p o", o=1)))
        p.emit("dve", ["bd_sb"], ["st_f"], lambda e: e.memset(st_f[:, :], 0.0))
        p.emit("dve", ["st_f"], ["st_b"], lambda e: e.tensor_copy(out=st_b[:, :], in_=st_f[:, :]))
        for blk in range(32):
            r0 = blk * 256

            def ld(dst, dk, src):
                p.dma("sp", [], [dk],
                      lambda e, src=src, dst=dst: e.dma_start(
                          out=dst[:, :].rearrange("p (c d) -> p c d", d=64),
                          in_=src.rearrange("(c p) d -> p c d", p=128)))
            ld(q_sb, "q_sb", qp[pr, r0:r0 + 256, :])
            ld(k_sb, "k_sb", kp[pr, r0:r0 + 256, :])
            ld(v_sb, "v_sb", vp[pr, r0:r0 + 256, :])
            for jc in range(2):
                p.emit("dve", ["k_sb", "kd_sb"], [("kdk", jc)],
                       lambda e, jc=jc: e.tensor_scalar(out=kdk[:, jc * 64:(jc + 1) * 64],
                                                        in0=k_sb[:, jc * 64:(jc + 1) * 64],
                                                        scalar1=kd_sb[:, jc:jc + 1], scalar2=None, op0=ALU.mult))
                pt, ptk = pT[jc]
                p.emit("pe", ["q_sb", "ident"], [ptk],
                       lambda e, pt=pt, jc=jc: e.transpose(pt[:64, :128], q_sb[:, jc * 64:(jc + 1) * 64], id_b[:, :]))
                p.emit("dve", [ptk], [("qT_sb", jc)],
                       lambda e, pt=pt, jc=jc: e.tensor_copy(out=qT_sb[:, jc * 128:(jc + 1) * 128], in_=pt[:64, :128]))
                p.emit("pe", ["k_sb", "ident"], [ptk],
                       lambda e, pt=pt, jc=jc: e.transpose(pt[:64, 128:256], k_sb[:, jc * 64:(jc + 1) * 64], id_b[:, :]))
                p.emit("dve", [ptk], [("kT_sb", jc)],
                       lambda e, pt=pt, jc=jc: e.tensor_copy(out=kT_sb[:, jc * 128:(jc + 1) * 128], in_=pt[:64, 128:256]))
            for ic in range(2):
                ps, psk = pN
                p.emit("pe", [("qT_sb", ic), "st_b"], [psk],
                       lambda e, ps=ps, ic=ic: e.matmul(ps[:, :64], qT_sb[:, ic * 128:(ic + 1) * 128],
                                                        st_b[:, :], start=True, stop=True))
                p.emit("dve", [psk, "qd_sb"], [("y_f", ic)],
                       lambda e, ps=ps, ic=ic: e.tensor_scalar(out=y_f[:, ic * 64:(ic + 1) * 64], in0=ps[:, 0:64],
                                                               scalar1=qd_sb[:, ic:ic + 1], scalar2=None, op0=ALU.mult))
            for jc in range(2):
                ps, psk = pA
                p.emit("pe", [("kT_sb", jc), ("qT_sb", 0), ("qT_sb", 1)], [psk],
                       lambda e, ps=ps, jc=jc: e.matmul(ps[:, :256], kT_sb[:, jc * 128:(jc + 1) * 128],
                                                        qT_sb[:64, :256], start=True, stop=True))
                p.emit("dve", [psk, "dd0", "dd1"], [("awT", jc)],
                       lambda e, ps=ps, jc=jc: e.tensor_tensor(out=awT[:, jc * 256:(jc + 1) * 256],
                                                               in0=ps[:, :256], in1=dd_sb[:, jc * 256:(jc + 1) * 256],
                                                               op=ALU.mult))
            for ic in range(2):
                ps, psk = pI[ic]
                for jc in range(2):
                    p.emit("pe", [("awT", jc), "v_sb"], [psk],
                           lambda e, ps=ps, ic=ic, jc=jc: e.matmul(ps[:, :64],
                                                                   awT[:, jc * 256 + ic * 128:jc * 256 + (ic + 1) * 128],
                                                                   v_sb[:, jc * 64:(jc + 1) * 64],
                                                                   start=(jc == 0), stop=(jc == 1)))
                p.emit("dve", [psk, ("y_f", ic)], [("y_o", ic)],
                       lambda e, ps=ps, ic=ic: e.scalar_tensor_tensor(out=y_o[:, ic * 64:(ic + 1) * 64],
                                                                      in0=ps[:, 0:64], scalar=1.0,
                                                                      in1=y_f[:, ic * 64:(ic + 1) * 64],
                                                                      op0=ALU.mult, op1=ALU.add))
            p.dma("pool", [("y_o", 0), ("y_o", 1)], [],
                  lambda e, pr=pr, r0=r0: e.dma_start(out=op[pr, r0:r0 + 256, :].rearrange("(c p) d -> p c d", p=128),
                                                      in_=y_o[:, :].rearrange("p (c d) -> p c d", d=64)))
            ps, psk = pU
            for jc in range(2):
                p.emit("pe", [("kdk", jc), "v_sb"], [psk],
                       lambda e, ps=ps, jc=jc: e.matmul(ps[:64, :64], kdk[:, jc * 64:(jc + 1) * 64],
                                                        v_sb[:, jc * 64:(jc + 1) * 64],
                                                        start=(jc == 0), stop=(jc == 1)))
            p.emit("dve", [psk, "bd_sb", "st_f"], ["st_f"],
                   lambda e, ps=ps: e.scalar_tensor_tensor(out=st_f[:, :], in0=st_f[:, :], scalar=bd_sb[:, :],
                                                           in1=ps[:64, 0:64], op0=ALU.mult, op1=ALU.add))
            p.emit("dve", ["st_f"], ["st_b"], lambda e: e.tensor_copy(out=st_b[:, :], in_=st_f[:, :]))



def build_c3(nc, p, stack):
    sb = stack.enter_context
    x = nc.dram_tensor("x", [NTOK, E], F32, kind="ExternalInput").ap()
    at = nc.dram_tensor("at", [NTOK, E], BF16, kind="ExternalInput").ap()
    lnTd = nc.dram_tensor("lnTd", [16, 128, NTOK], BF16, kind="ExternalInput").ap()
    nw = nc.dram_tensor("nw", [E], F32, kind="ExternalInput").ap()
    gwT = nc.dram_tensor("gwT", [E, E], F32, kind="ExternalInput").ap()
    owT = nc.dram_tensor("owT", [E, E], F32, kind="ExternalInput").ap()
    ident = nc.dram_tensor("ident", [128, 128], BF16, kind="ExternalInput").ap()
    ones1 = nc.dram_tensor("ones1", [1, 128], BF16, kind="ExternalInput").ap()
    x3 = nc.dram_tensor("x3", [NTOK, E], F32, kind="ExternalOutput").ap()

    x_t = sb(nc.sbuf_tensor("x_t", [128, E], F32))
    at_t = sb(nc.sbuf_tensor("at_t", [128, E], BF16))
    lnT_t = sb(nc.sbuf_tensor("lnT_t", [128, 16 * 128], BF16))
    gw_sb = sb(nc.sbuf_tensor("gw_sb", [128, 16 * E], BF16))
    ow_sb = sb(nc.sbuf_tensor("ow_sb", [128, 16 * E], BF16))
    nw_f = sb(nc.sbuf_tensor("nw_f", [1, E], F32))
    nw_b = sb(nc.sbuf_tensor("nw_b", [1, E], BF16))
    nwb = sb(nc.sbuf_tensor("nwb", [128, E], F32))
    on1 = sb(nc.sbuf_tensor("on1", [1, 128], BF16))
    id_b = sb(nc.sbuf_tensor("id_b", [128, 128], BF16))
    rn_sb = sb(nc.sbuf_tensor("rn_sb", [128, E], F32))
    g_tmp = sb(nc.sbuf_tensor("g_tmp", [128, 512], BF16))
    gr_sb = sb(nc.sbuf_tensor("gr_sb", [128, E], BF16))
    grT = sb(nc.sbuf_tensor("grT", [128, 16 * 128], BF16))
    x3_sb = sb(nc.sbuf_tensor("x3_sb", [128, E], F32))
    zb = sb(nc.sbuf_tensor("zb", [128, 1], F32))
    scr = {k: sb(nc.sbuf_tensor("scr_" + k, [128, d], F32)) for k, d in
           [("sq", E), ("s1", 1), ("sd", 1), ("rs", 1), ("eps", 1)]}
    pmm = [(sb(nc.psum_tensor(f"pmm{i}", [128, 512], F32)), f"pmm{i}") for i in range(2)]
    pbt = [(sb(nc.psum_tensor(f"pbt{i}", [128, 1024], BF16)), f"pbt{i}") for i in range(2)]

    p.emit("dve", [], ["eps"], lambda e: e.memset(scr["eps"][:, :], 1e-5))
    p.emit("dve", [], ["zb"], lambda e: e.memset(zb[:, :], 0.0))
    p.dma("sp", [], ["ident"], lambda e: e.dma_start(out=id_b[:, :], in_=ident[:, :]))
    p.dma("sp", [], ["on1"], lambda e: e.dma_start(out=on1[:, :], in_=ones1[:, :]))
    k_load_wcast(p, "pool", gw_sb, "gw_sb", gwT, 16, E, ndesc=4096)
    k_load_wcast(p, "pool", ow_sb, "ow_sb", owT, 16, E, ndesc=4096)
    p.dma("sp", [], ["nw_f"], lambda e: e.dma_start(out=nw_f[:, :], in_=nw.rearrange("(o e) -> o e", o=1)))
    p.emit("dve", ["nw_f"], ["nw_b"], lambda e: e.tensor_copy(out=nw_b[:, :], in_=nw_f[:, :]))
    for nch in range(4):
        ps, psk = pmm[nch % 2]
        p.emit("pe", ["nw_b", "on1"], [psk],
               lambda e, ps=ps, nch=nch: e.matmul(ps[:, :512], on1[0:1, :128], nw_b[0:1, nch * 512:(nch + 1) * 512],
                                                  start=True, stop=True))
        p.emit("act", [psk], [("nwb", nch)],
               lambda e, ps=ps, nch=nch: e.activation(out=nwb[:, nch * 512:(nch + 1) * 512], in_=ps[:, :512],
                                                      func=AF.Copy, bias=0.0, scale=1.0))

    for tt in range(NT):
        p.dma("sp", [], ["x_t"], lambda e, tt=tt: e.dma_start(out=x_t[:, :], in_=x[tt * 128:(tt + 1) * 128, :]))
        p.dma("sp", [], ["at_t"], lambda e, tt=tt: e.dma_start(out=at_t[:, :], in_=at[tt * 128:(tt + 1) * 128, :]))
        for kt in range(16):
            p.dma("sp", [], [("lnT_t", kt)],
                  lambda e, tt=tt, kt=kt: e.dma_start(out=lnT_t[:, kt * 128:(kt + 1) * 128],
                                                      in_=lnTd[kt, :, tt * 128:(tt + 1) * 128]))
        p.emit("act", ["at_t", "zb"], [("rms", 0)],
               lambda e: e.activation(out=scr["sq"][:, :], in_=at_t[:, :], func=AF.Square, bias=zb[:, :], scale=1.0))
        p.emit("dve", [("rms", 0)], [("rms", 1)], lambda e: e.reduce_sum(scr["s1"][:, :], scr["sq"][:, :], axis=AX.X))
        p.emit("act", [("rms", 1), "eps"], [("rms", 2)],
               lambda e: e.activation(out=scr["sd"][:, :], in_=scr["s1"][:, :], func=AF.Sqrt, bias=scr["eps"][:, :],
                                      scale=1.0 / E))
        p.emit("dve", [("rms", 2)], [("rms", 3)], lambda e: e.reciprocal(out=scr["rs"][:, :], in_=scr["sd"][:, :]))
        p.emit("dve", ["at_t", ("rms", 3)], ["rn_sb"],
               lambda e: e.tensor_scalar(out=rn_sb[:, :], in0=at_t[:, :], scalar1=scr["rs"][:, :], scalar2=None,
                                         op0=ALU.mult))
        p.emit("dve", ["rn_sb"] + [("nwb", n) for n in range(4)], ["rn_sb"],
               lambda e: e.tensor_tensor(out=rn_sb[:, :], in0=rn_sb[:, :], in1=nwb[:, :], op=ALU.mult))
        for nch in range(4):
            ps, psk = pmm[nch % 2]
            for kt in range(16):
                p.emit("pe", [("lnT_t", kt), ("gw_sb", kt)], [psk],
                       lambda e, ps=ps, kt=kt, nch=nch: e.matmul(ps[:, :512], lnT_t[:, kt * 128:(kt + 1) * 128],
                                                                 gw_sb[:, kt * E + nch * 512:kt * E + (nch + 1) * 512],
                                                                 start=(kt == 0), stop=(kt == 15)))
            p.emit("act", [psk, "zb"], ["g_tmp"],
                   lambda e, ps=ps: e.activation(out=g_tmp[:, :], in_=ps[:, :512], func=AF.Sigmoid, bias=zb[:, :],
                                                 scale=1.0))
            p.emit("dve", ["g_tmp", "rn_sb"], [("gr_sb", nch)],
                   lambda e, nch=nch: e.tensor_tensor(out=gr_sb[:, nch * 512:(nch + 1) * 512], in0=g_tmp[:, :],
                                                      in1=rn_sb[:, nch * 512:(nch + 1) * 512], op=ALU.mult))
        k_transpose16(p, gr_sb, [("gr_sb", i // 4) for i in range(16)], grT, "grT", id_b, pbt, 16)
        for nch in range(4):
            ps, psk = pmm[nch % 2]
            for kt in range(16):
                p.emit("pe", [("grT", kt), ("ow_sb", kt)], [psk],
                       lambda e, ps=ps, kt=kt, nch=nch: e.matmul(ps[:, :512], grT[:, kt * 128:(kt + 1) * 128],
                                                                 ow_sb[:, kt * E + nch * 512:kt * E + (nch + 1) * 512],
                                                                 start=(kt == 0), stop=(kt == 15)))
            p.emit("dve", [psk, "x_t"], [("x3_sb", nch)],
                   lambda e, ps=ps, nch=nch: e.tensor_tensor(out=x3_sb[:, nch * 512:(nch + 1) * 512], in0=ps[:, :512],
                                                             in1=x_t[:, nch * 512:(nch + 1) * 512], op=ALU.add))
        p.dma("pool", [("x3_sb", n) for n in range(4)], [],
              lambda e, tt=tt: e.dma_start(out=x3[tt * 128:(tt + 1) * 128, :], in_=x3_sb[:, :]))



def build_d1(nc, p, stack):
    sb = stack.enter_context
    O = 8192
    x = nc.dram_tensor("x", [NTOK, E], F32, kind="ExternalInput").ap()
    wg = nc.dram_tensor("wg", [E, O], I8, kind="ExternalInput").ap()
    wu = nc.dram_tensor("wu", [E, O], I8, kind="ExternalInput").ap()
    wd = nc.dram_tensor("wd", [O, E], I8, kind="ExternalInput").ap()
    sg = nc.dram_tensor("sg", [O], F32, kind="ExternalInput").ap()
    su = nc.dram_tensor("su", [O], F32, kind="ExternalInput").ap()
    sd = nc.dram_tensor("sd", [E], F32, kind="ExternalInput").ap()
    ident = nc.dram_tensor("ident", [128, 128], BF16, kind="ExternalInput").ap()
    ones1 = nc.dram_tensor("ones1", [1, 128], BF16, kind="ExternalInput").ap()
    x4 = nc.dram_tensor("x4", [NTOK, E], F32, kind="ExternalOutput").ap()

    x3_sb = sb(nc.sbuf_tensor("x3_sb", [128, 4 * E], F32))
    acc = sb(nc.sbuf_tensor("acc", [128, 4 * E], F32))
    lnT = sb(nc.sbuf_tensor("lnT", [128, 16 * 512], BF16))
    ln_b = sb(nc.sbuf_tensor("ln_b", [128, E], BF16))
    wg_c = [sb(nc.sbuf_tensor(f"wg_c{i}", [128, 16 * 512], BF16)) for i in range(2)]
    wu_c = [sb(nc.sbuf_tensor(f"wu_c{i}", [128, 16 * 512], BF16)) for i in range(2)]
    wd_c = [sb(nc.sbuf_tensor("wd_c0", [128, 4 * E], BF16))] * 2
    fuseT = sb(nc.sbuf_tensor("fuseT", [128, 4 * 512], BF16))
    g_tmp = sb(nc.sbuf_tensor("g_tmp", [128, 512], BF16))
    sg_sb = sb(nc.sbuf_tensor("sg_sb", [128, 64], F32))
    su_sb = sb(nc.sbuf_tensor("su_sb", [128, 64], F32))
    sd_f = sb(nc.sbuf_tensor("sd_f", [1, E], F32))
    sd_b = sb(nc.sbuf_tensor("sd_b", [1, E], BF16))
    sdb = sb(nc.sbuf_tensor("sdb", [128, E], F32))
    t2_sb = sb(nc.sbuf_tensor("t2_sb", [128, E], F32))
    on1 = sb(nc.sbuf_tensor("on1", [1, 128], BF16))
    id_b = sb(nc.sbuf_tensor("id_b", [128, 128], BF16))
    zb = sb(nc.sbuf_tensor("zb", [128, 1], F32))
    scr = {k: sb(nc.sbuf_tensor("scr_" + k, [128, d], F32)) for k, d in
           [("sq", E), ("s1", 1), ("s2", 1), ("nm", 1), ("sd", 1), ("rs", 1), ("eps", 1)]}
    pmm = [(sb(nc.psum_tensor(f"pmm{i}", [128, 512], F32)), f"pmm{i}") for i in range(4)]
    pbt = [(sb(nc.psum_tensor(f"pbt{i}", [128, 1024], BF16)), f"pbt{i}") for i in range(2)]

    p.emit("dve", [], ["eps"], lambda e: e.memset(scr["eps"][:, :], 1e-5))
    p.emit("dve", [], ["zb"], lambda e: e.memset(zb[:, :], 0.0))
    p.dma("sp", [], ["ident"], lambda e: e.dma_start(out=id_b[:, :], in_=ident[:, :]))
    p.dma("sp", [], ["on1"], lambda e: e.dma_start(out=on1[:, :], in_=ones1[:, :]))
    for ot in range(64):
        p.dma("sp", [], [("sg_sb", ot)],
              lambda e, ot=ot: e.dma_start(out=sg_sb[:, ot:ot + 1], in_=sg[ot * 128:(ot + 1) * 128].rearrange("(p o) -> p o", o=1)))
        p.dma("sp", [], [("su_sb", ot)],
              lambda e, ot=ot: e.dma_start(out=su_sb[:, ot:ot + 1], in_=su[ot * 128:(ot + 1) * 128].rearrange("(p o) -> p o", o=1)))
    p.dma("sp", [], ["sd_f"], lambda e: e.dma_start(out=sd_f[:, :], in_=sd.rearrange("(o e) -> o e", o=1)))
    p.emit("dve", ["sd_f"], ["sd_b"], lambda e: e.tensor_copy(out=sd_b[:, :], in_=sd_f[:, :]))
    for nch in range(4):
        ps, psk = pmm[nch % 2]
        p.emit("pe", ["sd_b", "on1"], [psk],
               lambda e, ps=ps, nch=nch: e.matmul(ps[:, :512], on1[0:1, :128], sd_b[0:1, nch * 512:(nch + 1) * 512],
                                                  start=True, stop=True))
        p.emit("act", [psk], [("sdb", nch)],
               lambda e, ps=ps, nch=nch: e.activation(out=sdb[:, nch * 512:(nch + 1) * 512], in_=ps[:, :512],
                                                      func=AF.Copy, bias=0.0, scale=1.0))

    for grp in range(4):
        for ttt in range(4):
            tt = grp * 4 + ttt
            p.dma("sp", [], [("x3_sb", ttt)],
                  lambda e, tt=tt, ttt=ttt: e.dma_start(out=x3_sb[:, ttt * E:(ttt + 1) * E],
                                                        in_=x[tt * 128:(tt + 1) * 128, :]))
            k_ln(p, ("x3_sb", ttt), x3_sb[:, ttt * E:(ttt + 1) * E], ln_b[:, :], "ln_b", scr, E)
            for kt in range(16):
                pb, pbk = pbt[kt % 2]
                p.emit("pe", ["ln_b", "ident"], [pbk],
                       lambda e, pb=pb, kt=kt: e.transpose(pb[:, :128], ln_b[:, kt * 128:(kt + 1) * 128], id_b[:, :]))
                p.emit("dve", [pbk], [("lnT", kt, ttt)],
                       lambda e, pb=pb, kt=kt, ttt=ttt: e.tensor_copy(
                           out=lnT[:, kt * 512 + ttt * 128:kt * 512 + (ttt + 1) * 128], in_=pb[:, :128]))
        for oc in range(16):
            cb = oc % 2
            wgc, wuc, wdc = wg_c[cb], wu_c[cb], wd_c[cb]
            for kt in range(16):
                p.dma("pool", [], [("wg_c", cb, kt)],
                      lambda e, oc=oc, kt=kt, wgc=wgc: e.dma_start(out=wgc[:, kt * 512:(kt + 1) * 512],
                                                                   in_=wg[kt * 128:(kt + 1) * 128, oc * 512:(oc + 1) * 512]))
                p.dma("pool", [], [("wu_c", cb, kt)],
                      lambda e, oc=oc, kt=kt, wuc=wuc: e.dma_start(out=wuc[:, kt * 512:(kt + 1) * 512],
                                                                   in_=wu[kt * 128:(kt + 1) * 128, oc * 512:(oc + 1) * 512]))
            for j in range(4):
                p.dma("pool", [], [("wd_c", 0, j)],
                      lambda e, oc=oc, j=j, wdc=wdc: e.dma_start(out=wdc[:, j * E:(j + 1) * E],
                                                                 in_=wd[oc * 512 + j * 128:oc * 512 + (j + 1) * 128, :]))
            for ot in range(4):
                ov = oc * 4 + ot
                psg, psgk = pmm[0]
                psu, psuk = pmm[1]
                for kt in range(16):
                    p.emit("pe", [("wg_c", cb, kt)] + [("lnT", kt, t3) for t3 in range(4)], [psgk],
                           lambda e, psg=psg, kt=kt, ot=ot, wgc=wgc: e.matmul(
                               psg[:, :512], wgc[:, kt * 512 + ot * 128:kt * 512 + (ot + 1) * 128],
                               lnT[:, kt * 512:(kt + 1) * 512], start=(kt == 0), stop=(kt == 15)))
                p.emit("act", [psgk, ("sg_sb", ov)], ["g_tmp"],
                       lambda e, psg=psg, ov=ov: e.activation(out=g_tmp[:, :], in_=psg[:, :512], func=AF.Silu,
                                                              bias=zb[:, :], scale=sg_sb[:, ov:ov + 1]))
                for kt in range(16):
                    p.emit("pe", [("wu_c", cb, kt)] + [("lnT", kt, t3) for t3 in range(4)], [psuk],
                           lambda e, psu=psu, kt=kt, ot=ot, wuc=wuc: e.matmul(
                               psu[:, :512], wuc[:, kt * 512 + ot * 128:kt * 512 + (ot + 1) * 128],
                               lnT[:, kt * 512:(kt + 1) * 512], start=(kt == 0), stop=(kt == 15)))
                p.emit("dve", [psuk, "g_tmp", ("su_sb", ov)], [("fuseT", ot)],
                       lambda e, psu=psu, ov=ov, ot=ot: e.scalar_tensor_tensor(
                           out=fuseT[:, ot * 512:(ot + 1) * 512], in0=psu[:, :512], scalar=su_sb[:, ov:ov + 1],
                           in1=g_tmp[:, :], op0=ALU.mult, op1=ALU.mult))
            for ttt in range(4):
                for nch in range(4):
                    ps, psk = pmm[2 + nch % 2]
                    for ot in range(4):
                        p.emit("pe", [("fuseT", ot), ("wd_c", 0, ot)], [psk],
                               lambda e, ps=ps, ot=ot, ttt=ttt, nch=nch, wdc=wdc: e.matmul(
                                   ps[:, :512], fuseT[:, ot * 512 + ttt * 128:ot * 512 + (ttt + 1) * 128],
                                   wdc[:, ot * E + nch * 512:ot * E + (nch + 1) * 512],
                                   start=(ot == 0), stop=(ot == 3)))
                    if oc == 0:
                        p.emit("act", [psk], [("acc", ttt, nch)],
                               lambda e, ps=ps, ttt=ttt, nch=nch: e.activation(
                                   out=acc[:, ttt * E + nch * 512:ttt * E + (nch + 1) * 512], in_=ps[:, :512],
                                   func=AF.Copy, bias=0.0, scale=1.0))
                    else:
                        p.emit("dve", [psk, ("acc", ttt, nch)], [("acc", ttt, nch)],
                               lambda e, ps=ps, ttt=ttt, nch=nch: e.tensor_tensor(
                                   out=acc[:, ttt * E + nch * 512:ttt * E + (nch + 1) * 512], in0=ps[:, :512],
                                   in1=acc[:, ttt * E + nch * 512:ttt * E + (nch + 1) * 512], op=ALU.add))
        for ttt in range(4):
            tt = grp * 4 + ttt
            p.emit("dve", [("acc", ttt, n) for n in range(4)] + [("sdb", n) for n in range(4)], ["t2_sb"],
                   lambda e, ttt=ttt: e.tensor_tensor(out=t2_sb[:, :], in0=acc[:, ttt * E:(ttt + 1) * E],
                                                      in1=sdb[:, :], op=ALU.mult))
            p.emit("dve", ["t2_sb", ("x3_sb", ttt)], ["t2_sb"],
                   lambda e, ttt=ttt: e.tensor_tensor(out=t2_sb[:, :], in0=t2_sb[:, :],
                                                      in1=x3_sb[:, ttt * E:(ttt + 1) * E], op=ALU.add))
            p.dma("pool", ["t2_sb"], [],
                  lambda e, tt=tt: e.dma_start(out=x4[tt * 128:(tt + 1) * 128, :], in_=t2_sb[:, :]))



def rope_tables():
    pos = np.arange(S, dtype=np.float32)[:, None]
    exps = (-np.arange(0, 128, 2, dtype=np.float32) / 128.0)
    thetas = np.repeat(np.power(np.float32(10000.0), exps), 2)
    ang = pos * thetas[None, :]
    cos = np.cos(ang).astype(np.float32)
    sin = np.sin(ang).astype(np.float32)
    return np.tile(cos, (1, 16)), np.tile(sin, (1, 16))


def decay_buffers():
    H, BS = 32, 256
    base = 1.0 / 2.0 ** (8.0 / H)
    rate = base ** (np.arange(H, dtype=np.float32) + 1.0)
    factor = 1.0 - 0.0 / (32 - 1 + 1e-5) + 1e-5
    slope = (rate * factor).astype(np.float32)[:, None, None]
    r = np.arange(BS, dtype=np.float32) + 1.0
    qd = np.exp(-slope * r[:, None]).astype(np.float32)
    kd = np.exp(-slope * (BS - r[:, None])).astype(np.float32)
    ddm = r[:, None] - r[None, :]
    ddm = slope * ddm[None, :, :][0]
    dd = np.where(ddm >= 0, -ddm, -np.inf)
    dd = np.exp(dd).astype(np.float32)
    bd = np.exp(-slope[:, 0, 0] * BS).astype(np.float32)
    qd_pc = qd[:, :, 0].reshape(H, 2, 128).transpose(0, 2, 1).copy()
    kd_pc = kd[:, :, 0].reshape(H, 2, 128).transpose(0, 2, 1).copy()
    ddT = dd.transpose(0, 2, 1).reshape(H, 2, 128, 256).copy()
    return qd_pc, kd_pc, ddT, bd


def masksT():
    mk = np.zeros((4, 128, 512), dtype=np.float32)
    kk = np.arange(128)[:, None]
    qq = np.arange(512)[None, :]
    for m in range(4):
        mk[m] = np.where(kk <= qq - m * 128, 0.0, -1e30)
    return mk



def kernel(**inp):
    ident = np.eye(128, dtype=NPBF16)
    ones1 = np.ones((1, 128), dtype=NPBF16)
    onesc = np.ones((128, 1), dtype=NPBF16)
    cosr, sinr = rope_tables()
    mk = masksT()

    x_flat = np.ascontiguousarray(inp["L0_attn_x"].reshape(T, E).astype(np.float32))

    def shard_rows(a):
        return [np.ascontiguousarray(a[c * NTOK:(c + 1) * NTOK]) for c in range(NCORE)]

    maps = []
    for c in range(NCORE):
        pos0 = (c * NTOK) % S
        maps.append({"x": shard_rows(x_flat)[c],
                     "Wq": inp["L0_attn_Wq"], "Wk": inp["L0_attn_Wk"], "Wv": inp["L0_attn_Wv"],
                     "bq": inp["L0_attn_bq"], "bk": inp["L0_attn_bk"], "bv": inp["L0_attn_bv"],
                     "cosr": np.ascontiguousarray(cosr[pos0:pos0 + NTOK]),
                     "sinr": np.ascontiguousarray(sinr[pos0:pos0 + NTOK]),
                     "ident": ident, "ones1": ones1})
    r = run_launch(build_a1, maps)
    q_all = np.concatenate([rr["qo"] for rr in r], axis=0).reshape(2, S, 16, 128)
    k_all = np.concatenate([rr["ko"] for rr in r], axis=0).reshape(2, S, 16, 128)
    v_all = np.concatenate([rr["vo"] for rr in r], axis=0).reshape(2, S, 16, 128)

    maps = []
    for c in range(NCORE):
        prs = [(idx // 16, idx % 16) for idx in range(c * 4, c * 4 + 4)]
        maps.append({"qp": np.ascontiguousarray(np.stack([q_all[b, :, h, :] for b, h in prs])),
                     "kp": np.ascontiguousarray(np.stack([k_all[b, :, h, :] for b, h in prs])),
                     "vp": np.ascontiguousarray(np.stack([v_all[b, :, h, :] for b, h in prs])),
                     "maskT": mk, "onesc": onesc, "ones1": ones1})
    r = run_launch(build_a2, maps)
    at4 = np.empty((2, S, 16, 128), dtype=NPBF16)
    for c in range(NCORE):
        for j, idx in enumerate(range(c * 4, c * 4 + 4)):
            at4[idx // 16, :, idx % 16, :] = r[c]["op"][j]
    attn_flat = np.ascontiguousarray(at4.reshape(T, E))

    gwT0 = np.ascontiguousarray(inp["L0_ffn_gate_w"].T.astype(np.float32))
    maps = []
    for c in range(NCORE):
        maps.append({"x": shard_rows(x_flat)[c], "at": shard_rows(attn_flat)[c],
                     "Wo": inp["L0_attn_Wo"], "bo": inp["L0_attn_bo"], "gwT": gwT0,
                     "ident": ident, "ones1": ones1})
    r = run_launch(build_a3, maps)
    x1_sh = [rr["x1"] for rr in r]
    ln2T_all = np.concatenate([rr["lnTo"] for rr in r], axis=2)
    cw_all = np.concatenate([rr["cwo"] for rr in r], axis=0)

    maps = []
    for c in range(NCORE):
        maps.append({"w1": np.ascontiguousarray(inp["L0_ffn_w1"][c].astype(np.float32)),
                     "w2": np.ascontiguousarray(inp["L0_ffn_w2"][c].astype(np.float32)),
                     "lnTd": ln2T_all, "cwc": np.ascontiguousarray(cw_all[:, c])})
    r = run_launch(build_b1, maps)
    y_ex = [rr["yo"] for rr in r]

    sw1T = np.ascontiguousarray(inp["L0_ffn_sw1"].T.astype(np.float32))
    sw2T = np.ascontiguousarray(inp["L0_ffn_sw2"].T.astype(np.float32))
    sw3T = np.ascontiguousarray(inp["L0_ffn_sw3"].T.astype(np.float32))
    maps = []
    for c in range(NCORE):
        maps.append({"x1": x1_sh[c],
                     "lnTd": np.ascontiguousarray(ln2T_all[:, :, c * NTOK:(c + 1) * NTOK]),
                     "ye": np.ascontiguousarray(np.stack([y_ex[e][c * NTOK:(c + 1) * NTOK] for e in range(8)])),
                     "sw1T": sw1T, "sw2T": sw2T, "sw3T": sw3T, "ident": ident})
    r = run_launch(build_b2, maps)
    x2_sh = [rr["x2"] for rr in r]

    qkvT = np.ascontiguousarray(inp["L1_attn_qkv_w"].T.astype(np.float32))
    maps = [{"x": x2_sh[c], "wT": qkvT, "ident": ident} for c in range(NCORE)]
    r = run_launch(build_c1, maps)
    qkv_all = np.concatenate([rr["qkv"] for rr in r], axis=0).reshape(2, S, 32, 192)
    ln3T_sh = [rr["lnTo"] for rr in r]

    qd_pc, kd_pc, ddT, bd = decay_buffers()
    maps = []
    for c in range(NCORE):
        idxs = list(range(c * 8, c * 8 + 8))
        hs = [i % 32 for i in idxs]
        maps.append({"qp": np.ascontiguousarray(np.stack([qkv_all[i // 32, :, i % 32, 0:64] for i in idxs])),
                     "kp": np.ascontiguousarray(np.stack([qkv_all[i // 32, :, i % 32, 64:128] for i in idxs])),
                     "vp": np.ascontiguousarray(np.stack([qkv_all[i // 32, :, i % 32, 128:192] for i in idxs])),
                     "ddT": np.ascontiguousarray(ddT[hs]), "qd": np.ascontiguousarray(qd_pc[hs]),
                     "kd": np.ascontiguousarray(kd_pc[hs]),
                     "bdc": np.ascontiguousarray(np.repeat(bd[hs][:, None], 64, axis=1)),
                     "ident": ident})
    r = run_launch(build_c2, maps)
    at1 = np.empty((2, S, 32, 64), dtype=NPBF16)
    for c in range(NCORE):
        for j, i in enumerate(range(c * 8, c * 8 + 8)):
            at1[i // 32, :, i % 32, :] = r[c]["op"][j]
    attn1_flat = np.ascontiguousarray(at1.reshape(T, E))

    gwT1 = np.ascontiguousarray(inp["L1_attn_gate_w"].T.astype(np.float32))
    owT1 = np.ascontiguousarray(inp["L1_attn_out_w"].T.astype(np.float32))
    maps = []
    for c in range(NCORE):
        maps.append({"x": x2_sh[c], "at": shard_rows(attn1_flat)[c], "lnTd": ln3T_sh[c],
                     "nw": inp["L1_attn_norm_w"].astype(np.float32), "gwT": gwT1, "owT": owT1,
                     "ident": ident, "ones1": ones1})
    r = run_launch(build_c3, maps)
    x3_sh = [rr["x3"] for rr in r]

    maps = []
    for c in range(NCORE):
        maps.append({"x": x3_sh[c], "wg": inp["L1_ffn_w_gate"], "wu": inp["L1_ffn_w_up"],
                     "wd": inp["L1_ffn_w_down"], "sg": inp["L1_ffn_s_gate"].astype(np.float32),
                     "su": inp["L1_ffn_s_up"].astype(np.float32), "sd": inp["L1_ffn_s_down"].astype(np.float32),
                     "ident": ident, "ones1": ones1})
    r = run_launch(build_d1, maps)
    out = np.concatenate([rr["x4"] for rr in r], axis=0)
    return out.reshape(2, S, E).astype(np.float32)


# version 02 (final)
# speedup vs baseline: 1.2002x; 1.2002x over previous
import numpy as np
import ml_dtypes
from contextlib import ExitStack
import concourse.bass as bass
import concourse.mybir as mybir
from concourse.bass_utils import run_bass_kernel_spmd

F32 = mybir.dt.float32
BF16 = mybir.dt.bfloat16
I8 = mybir.dt.int8
AF = mybir.ActivationFunctionType
ALU = mybir.AluOpType
AX = mybir.AxisListType
NPBF16 = ml_dtypes.bfloat16

ENGS = ["pe", "act", "dve", "pool", "sp"]
EMAP = {"pe": "tensor", "act": "scalar", "dve": "vector", "pool": "gpsimd", "sp": "sync"}
DMA_QS = {"sp": 8, "pool": 8, "act": 4}

NCORE = 8
E = 2048
T = 16384
S = 8192
NTOK = T // NCORE
NT = NTOK // 128


class Prog:

    def __init__(self, nc, stack):
        self.nc = nc
        self.sems = {e: stack.enter_context(nc.semaphore(name=f"s_{e}")) for e in ENGS}
        self.dsems = {q: [stack.enter_context(nc.semaphore(name=f"d_{q}{i}"))
                          for i in range(n)] for q, n in DMA_QS.items()}
        self.cnt = {e: 0 for e in ENGS}
        self.dcnt = {q: [0] * n for q, n in DMA_QS.items()}
        self.drr = {q: 0 for q in DMA_QS}
        self.clock = {e: {} for e in ENGS}
        self.snap = {}
        self.writers = {}
        self.readers = {}
        self.ops = {e: [] for e in ENGS}

    def _semobj(self, key):
        if isinstance(key, str):
            return self.sems[key]
        q, _, si = key
        return self.dsems[q][si]

    def _need(self, reads, writes):
        need = {}

        def add(m):
            k, v = m
            if need.get(k, 0) < v:
                need[k] = v
        for b in reads:
            for m in self.writers.get(b, ()):
                add(m)
        for b in writes:
            for m in self.writers.get(b, ()):
                add(m)
            for m in self.readers.get(b, ()):
                add(m)
        return need

    SELF_GAP = 4

    def _waits(self, e, need):
        ck = self.clock[e]
        waits = []
        for k, v in sorted(need.items(), key=lambda kv: str(kv[0])):
            if k == e:
                if v > self.cnt[e] - self.SELF_GAP and ck.get(k, 0) < v:
                    waits.append((k, v))
                    ck[k] = v
                continue
            if ck.get(k, 0) >= v:
                continue
            waits.append((k, v))
            sn = self.snap.get((k, v))
            if sn:
                for k2, v2 in sn.items():
                    if ck.get(k2, 0) < v2:
                        ck[k2] = v2
            if ck.get(k, 0) < v:
                ck[k] = v
        return waits

    def _record(self, m, reads, writes):
        for b in writes:
            self.writers[b] = [m]
            self.readers[b] = []
        for b in reads:
            self.readers.setdefault(b, []).append(m)

    def emit(self, e, reads, writes, fn):
        waits = self._waits(e, self._need(reads, writes))
        self.cnt[e] += 1
        m = (e, self.cnt[e])
        sn = dict(self.clock[e])
        sn[e] = self.cnt[e]
        self.snap[m] = sn
        self.ops[e].append((waits, fn, (e, 1)))
        self._record(m, reads, writes)
        return m

    def dma(self, q, reads, writes, fn):
        need = self._need(reads, writes)
        si = self.drr[q]
        self.drr[q] = (si + 1) % DMA_QS[q]
        key = (q, "d", si)
        prev = self.dcnt[q][si]
        if prev > 0 and need.get(key, 0) < 16 * prev:
            need[key] = 16 * prev
        waits = self._waits(q, need)
        self.dcnt[q][si] += 1
        m = (key, 16 * self.dcnt[q][si])
        sn = dict(self.clock[q])
        sn[key] = m[1]
        self.snap[m] = sn
        self.ops[q].append((waits, fn, (key, 16)))
        self._record(m, reads, writes)
        return m

    def finish(self):
        waits = []
        for e in ENGS:
            if e != "sp" and self.cnt[e] > 0:
                waits.append((e, self.cnt[e]))
        for q, cnts in self.dcnt.items():
            for si, c in enumerate(cnts):
                if c > 0:
                    waits.append(((q, "d", si), 16 * c))
        self.ops["sp"].append((waits, None, None))

    def build(self):
        self.finish()
        with self.nc.Block() as block:
            for e in ENGS:
                def mk(e):
                    def body(eng):
                        for waits, fn, inc in self.ops[e]:
                            for (k, v) in waits:
                                eng.wait_ge(self._semobj(k), v)
                            if fn is not None:
                                inst = fn(eng)
                                inst.then_inc(self._semobj(inc[0]), inc[1])
                    return body
                getattr(block, EMAP[e])(mk(e))


def new_nc():
    return bass.Bass("TRN2", target_bir_lowering=False)


def run_launch(builder, in_maps, **kw):
    nc = new_nc()
    with ExitStack() as stack:
        p = Prog(nc, stack)
        builder(nc, p, stack)
        p.build()
    res = run_bass_kernel_spmd(nc, in_maps, core_ids=list(range(NCORE)), **kw)
    return res.results



def k_load_wcast(p, q, dst, dname, src, KT, N, ndesc=None):
    for kt in range(KT):
        p.dma(q, [], [(dname, kt)],
              lambda e, kt=kt: e.dma_start(out=dst[:, kt * N:(kt + 1) * N],
                                           in_=src[kt * 128:(kt + 1) * 128, :],
                                           max_dma_last_dim=ndesc))


def k_ln(p, xk, x_ap, out_ap, outk, scr, D, extra_reads=()):
    sq, s1, nm, sd, rs, eps = scr["sq"], scr["s1"], scr["nm"], scr["sd"], scr["rs"], scr["eps"]
    s2 = scr["s2"]
    er = list(extra_reads)
    p.emit("dve", [xk] + er, [("lnst", 0)], lambda e: e.reduce_sum(s1[:, :], x_ap, axis=AX.X))
    p.emit("dve", [("lnst", 0)], [("lnst", 1)],
           lambda e: e.tensor_scalar(out=nm[:, :], in0=s1[:, :], scalar1=-1.0 / D, scalar2=None, op0=ALU.mult))
    p.emit("act", [xk, ("lnst", 1)] + er, [("lnst", 2)],
           lambda e: e.activation(out=sq[:, :], in_=x_ap, func=AF.Square, bias=nm[:, :], scale=1.0))
    p.emit("dve", [("lnst", 2)], [("lnst", 3)], lambda e: e.reduce_sum(s2[:, :], sq[:, :], axis=AX.X))
    p.emit("act", [("lnst", 3), "eps"], [("lnst", 4)],
           lambda e: e.activation(out=sd[:, :], in_=s2[:, :], func=AF.Sqrt, bias=eps[:, :], scale=1.0 / D))
    p.emit("dve", [("lnst", 4)], [("lnst", 5)], lambda e: e.reciprocal(out=rs[:, :], in_=sd[:, :]))
    p.emit("dve", [xk, ("lnst", 1), ("lnst", 5)] + er, [outk],
           lambda e: e.tensor_scalar(out=out_ap, in0=x_ap, scalar1=nm[:, :], scalar2=rs[:, :],
                                     op0=ALU.add, op1=ALU.mult))


def k_transpose16(p, src, srck, dst, dstname, ident, pbt, n, dst_off=0):
    for i in range(n):
        pb, pbk = pbt[i % 2]
        p.emit("pe", [srck if not isinstance(srck, list) else srck[i], "ident"], [pbk],
               lambda e, i=i, pb=pb: e.transpose(pb[:, :128], src[:, i * 128:(i + 1) * 128], ident[:, :]))
        p.emit("dve", [pbk], [(dstname, dst_off + i)],
               lambda e, i=i, pb=pb: e.tensor_copy(out=dst[:, (dst_off + i) * 128:(dst_off + i + 1) * 128],
                                                   in_=pb[:, :128]))



A1_DEBUG = False

def build_a1(nc, p, stack):
    sb = stack.enter_context
    x = nc.dram_tensor("x", [NTOK, E], F32, kind="ExternalInput").ap()
    Ws = {w: nc.dram_tensor(w, [E, E], F32, kind="ExternalInput").ap() for w in ["Wq", "Wk", "Wv"]}
    Bs = {b: nc.dram_tensor(b, [E], F32, kind="ExternalInput").ap() for b in ["bq", "bk", "bv"]}
    cosr = nc.dram_tensor("cosr", [NTOK, E], F32, kind="ExternalInput").ap()
    sinr = nc.dram_tensor("sinr", [NTOK, E], F32, kind="ExternalInput").ap()
    ident = nc.dram_tensor("ident", [128, 128], BF16, kind="ExternalInput").ap()
    ones1 = nc.dram_tensor("ones1", [1, 128], BF16, kind="ExternalInput").ap()
    outs = {w: nc.dram_tensor(w, [NTOK, E], BF16, kind="ExternalOutput").ap()
            for w in ["qo", "ko", "vo"]}
    if A1_DEBUG:
        lno = nc.dram_tensor("lno", [NTOK, E], BF16, kind="ExternalOutput").ap()
        lnTo = nc.dram_tensor("lnTo", [128, NT * 16 * 128], BF16, kind="ExternalOutput").ap()
        stto = nc.dram_tensor("stto", [NTOK, 8], F32, kind="ExternalOutput").ap()
        stt_sb = sb(nc.sbuf_tensor("stt_sb", [128, 8], F32))

    x_t = [sb(nc.sbuf_tensor(f"x_t{i}", [128, E], F32)) for i in range(2)]
    ln_b = [sb(nc.sbuf_tensor(f"ln_b{i}", [128, E], BF16)) for i in range(2)]
    lnT = sb(nc.sbuf_tensor("lnT", [128, NT * 16 * 128], BF16))
    w_sb = sb(nc.sbuf_tensor("w_sb", [128, 16 * E], BF16))
    b_b = sb(nc.sbuf_tensor("b_b", [1, E], BF16))
    on1 = sb(nc.sbuf_tensor("on1", [1, 128], BF16))
    id_b = sb(nc.sbuf_tensor("id_b", [128, 128], BF16))
    cos_t = [sb(nc.sbuf_tensor("cos_t", [128, E], F32))] * 2
    sin_t = [sb(nc.sbuf_tensor("sin_t", [128, E], F32))] * 2
    out_f = sb(nc.sbuf_tensor("out_f", [128, E], F32))
    out_b = [sb(nc.sbuf_tensor(f"out_b{i}", [128, E], BF16)) for i in range(2)]
    r1 = sb(nc.sbuf_tensor("r1", [128, E // 2], F32))
    r2 = sb(nc.sbuf_tensor("r2", [128, E // 2], F32))
    scr = {"sq": sb(nc.sbuf_tensor("sq", [128, E], F32)),
           "s1": sb(nc.sbuf_tensor("s1", [128, 1], F32)),
           "s2": sb(nc.sbuf_tensor("s2", [128, 1], F32)),
           "nm": sb(nc.sbuf_tensor("nm", [128, 1], F32)),
           "sd": sb(nc.sbuf_tensor("sd", [128, 1], F32)),
           "rs": sb(nc.sbuf_tensor("rs", [128, 1], F32)),
           "eps": sb(nc.sbuf_tensor("eps", [128, 1], F32))}
    pmm = [(sb(nc.psum_tensor(f"pmm{i}", [128, 512], F32)), f"pmm{i}") for i in range(2)]
    pbt = [(sb(nc.psum_tensor(f"pbt{i}", [128, 1024], BF16)), f"pbt{i}") for i in range(2)]

    p.emit("dve", [], ["eps"], lambda e: e.memset(scr["eps"][:, :], 1e-5))
    p.dma("sp", [], ["ident"], lambda e: e.dma_start(out=id_b[:, :], in_=ident[:, :]))
    p.dma("sp", [], ["on1"], lambda e: e.dma_start(out=on1[:, :], in_=ones1[:, :]))

    for tt in range(NT):
        xb = x_t[tt % 2]
        xk = ("x_t", tt % 2)
        p.dma("sp", [], [xk], lambda e, tt=tt, xb=xb: e.dma_start(out=xb[:, :], in_=x[tt * 128:(tt + 1) * 128, :]))
        lb = ln_b[tt % 2]
        lk = ("ln_b", tt % 2)
        k_ln(p, xk, xb[:, :], lb[:, :], lk, scr, E)
        if A1_DEBUG:
            p.dma("pool", [lk], [], lambda e, tt=tt, lb=lb: e.dma_start(out=lno[tt * 128:(tt + 1) * 128, :], in_=lb[:, :]))
            for ci, (tile, kk) in enumerate([(scr["s1"], ("lnst", 0)), (scr["nm"], ("lnst", 1)),
                                             (scr["s2"], ("lnst", 3)), (scr["sd"], ("lnst", 4)),
                                             (scr["rs"], ("lnst", 5))]):
                p.emit("dve", [kk], [("stt_sb", ci)],
                       lambda e, ci=ci, tile=tile: e.tensor_copy(out=stt_sb[:, ci:ci + 1], in_=tile[:, :]))
            p.dma("pool", [("stt_sb", ci) for ci in range(5)], [],
                  lambda e, tt=tt: e.dma_start(out=stto[tt * 128:(tt + 1) * 128, :], in_=stt_sb[:, :]))
        k_transpose16(p, lb, lk, lnT, "lnT", id_b, pbt, 16, dst_off=tt * 16)
        if A1_DEBUG:
            for kt in range(16):
                p.dma("pool", [("lnT", tt * 16 + kt)], [],
                      lambda e, tt=tt, kt=kt: e.dma_start(out=lnTo[:, (tt * 16 + kt) * 128:(tt * 16 + kt + 1) * 128],
                                                          in_=lnT[:, (tt * 16 + kt) * 128:(tt * 16 + kt + 1) * 128]))

    for wi, wname in enumerate(["Wq", "Wk", "Wv"]):
        bname = "b" + wname[1]
        oname = wname[1] + "o"
        k_load_wcast(p, "pool", w_sb, "w_sb", Ws[wname], 16, E, ndesc=4096)
        p.dma("pool", [], ["b_b"], lambda e, bname=bname: e.dma_start(out=b_b[:, :], in_=Bs[bname].rearrange("(o e) -> o e", o=1)))
        for tt in range(NT):
            rope = wname != "Wv"
            if rope:
                ck = "cos_t"
                sk = "sin_t"
                p.dma("sp", [], [ck], lambda e, tt=tt: e.dma_start(out=cos_t[0][:, :], in_=cosr[tt * 128:(tt + 1) * 128, :]))
                p.dma("sp", [], [sk], lambda e, tt=tt: e.dma_start(out=sin_t[0][:, :], in_=sinr[tt * 128:(tt + 1) * 128, :]))
            for nch in range(4):
                ps, psk = pmm[nch % 2]
                p.emit("pe", ["b_b", "on1"], [psk],
                       lambda e, ps=ps, nch=nch: e.matmul(ps[:, :512], on1[0:1, :128], b_b[0:1, nch * 512:(nch + 1) * 512],
                                                          start=True, stop=False))
                for kt in range(16):
                    p.emit("pe", [("lnT", tt * 16 + kt), ("w_sb", kt)], [psk],
                           lambda e, ps=ps, tt=tt, kt=kt, nch=nch: e.matmul(
                               ps[:, :512], lnT[:, (tt * 16 + kt) * 128:(tt * 16 + kt + 1) * 128],
                               w_sb[:, kt * E + nch * 512:kt * E + (nch + 1) * 512],
                               start=False, stop=(kt == 15)))
                if rope:
                    p.emit("act", [psk], [("out_f", nch)],
                           lambda e, ps=ps, nch=nch: e.activation(out=out_f[:, nch * 512:(nch + 1) * 512],
                                                                  in_=ps[:, :512], func=AF.Copy, bias=0.0, scale=1.0))
                else:
                    p.emit("act", [psk], [("out_b", tt % 2, nch)],
                           lambda e, ps=ps, nch=nch, tt=tt: e.activation(out=out_b[tt % 2][:, nch * 512:(nch + 1) * 512],
                                                                         in_=ps[:, :512], func=AF.Copy, bias=0.0, scale=1.0))
            ob = out_b[tt % 2]
            obks = [("out_b", tt % 2, n) for n in range(4)]
            if rope:
                def vw(t2, i):
                    return t2[:, :].rearrange("p (n t) -> p n t", t=2)[:, :, i:i + 1]
                qe, qo = vw(out_f, 0), vw(out_f, 1)
                ce, co = vw(cos_t[0], 0), vw(cos_t[0], 1)
                se, so = vw(sin_t[0], 0), vw(sin_t[0], 1)
                r1v = r1[:, :].rearrange("p (n t) -> p n t", t=1)
                r2v = r2[:, :].rearrange("p (n t) -> p n t", t=1)
                ofk = [("out_f", n) for n in range(4)]
                p.emit("dve", ofk + [ck], ["r1"], lambda e, qe=qe, ce=ce: e.tensor_tensor(out=r1v, in0=qe, in1=ce, op=ALU.mult))
                p.emit("dve", ofk + [sk], ["r2"], lambda e, qo=qo, se=se: e.tensor_tensor(out=r2v, in0=qo, in1=se, op=ALU.mult))
                p.emit("dve", ["r1", "r2"], [obks[0]],
                       lambda e, ob=ob: e.tensor_tensor(out=vw(ob, 0), in0=r1v, in1=r2v, op=ALU.subtract))
                p.emit("dve", ofk + [ck], ["r1"], lambda e, qo=qo, co=co: e.tensor_tensor(out=r1v, in0=qo, in1=co, op=ALU.mult))
                p.emit("dve", ofk + [sk], ["r2"], lambda e, qe=qe, so=so: e.tensor_tensor(out=r2v, in0=qe, in1=so, op=ALU.mult))
                p.emit("dve", ["r1", "r2"], [obks[1]],
                       lambda e, ob=ob: e.tensor_tensor(out=vw(ob, 1), in0=r1v, in1=r2v, op=ALU.add))
                obks = obks[:2]
            p.dma("pool", obks, [],
                  lambda e, ob=ob, tt=tt, oname=oname: e.dma_start(out=outs[oname][tt * 128:(tt + 1) * 128, :], in_=ob[:, :]))



def build_a2(nc, p, stack):
    sb = stack.enter_context
    NP = 4
    qp = nc.dram_tensor("qp", [NP, S, 128], BF16, kind="ExternalInput").ap()
    kp = nc.dram_tensor("kp", [NP, S, 128], BF16, kind="ExternalInput").ap()
    vp = nc.dram_tensor("vp", [NP, S, 128], BF16, kind="ExternalInput").ap()
    maskT = nc.dram_tensor("maskT", [4, 128, 512], F32, kind="ExternalInput").ap()
    onesc = nc.dram_tensor("onesc", [128, 1], BF16, kind="ExternalInput").ap()
    ones1 = nc.dram_tensor("ones1", [1, 128], BF16, kind="ExternalInput").ap()
    op = nc.dram_tensor("op", [NP, S, 128], BF16, kind="ExternalOutput").ap()

    qT = sb(nc.sbuf_tensor("qT", [128, S], BF16))
    kT = sb(nc.sbuf_tensor("kT", [128, S], BF16))
    v_sb = sb(nc.sbuf_tensor("v_sb", [128, S], BF16))
    pT = [sb(nc.sbuf_tensor(f"pT{i}", [128, 512], BF16)) for i in range(2)]
    dg = sb(nc.sbuf_tensor("dg", [128, 512], F32))
    mk_sb = sb(nc.sbuf_tensor("mk_sb", [128, 4 * 512], F32))
    onc = sb(nc.sbuf_tensor("onc", [128, 1], BF16))
    on1 = sb(nc.sbuf_tensor("on1", [1, 128], BF16))
    zb = sb(nc.sbuf_tensor("zb", [128, 1], F32))
    rinv = sb(nc.sbuf_tensor("rinv", [128, 4], F32))
    o_b = sb(nc.sbuf_tensor("o_b", [128, 4 * 128], BF16))

    rs_sb = sb(nc.sbuf_tensor("rs_sb", [1, 512], F32))
    onesf = sb(nc.sbuf_tensor("onesf", [1, 1], F32))
    pS2 = [(sb(nc.psum_tensor(f"pS{i}", [128, 512], F32)), f"pS{i}") for i in range(2)]
    pO = [(sb(nc.psum_tensor(f"pO{i}", [128, 512], F32)), f"pO{i}") for i in range(4)]
    pR = (sb(nc.psum_tensor("pR", [128, 512], F32)), "pR")
    pQ = (sb(nc.psum_tensor("pQ", [128, 512], F32)), "pQ")

    sc = 1.0 / np.sqrt(128.0)
    p.emit("dve", [], ["zb"], lambda e: e.memset(zb[:, :], 0.0))
    p.emit("dve", [], ["onesf"], lambda e: e.memset(onesf[:, :], 1.0))
    p.dma("sp", [], ["onc"], lambda e: e.dma_start(out=onc[:, :], in_=onesc[:, :]))
    p.dma("sp", [], ["on1"], lambda e: e.dma_start(out=on1[:, :], in_=ones1[:, :]))
    for m in range(4):
        p.dma("sp", [], [("mk_sb", m)],
              lambda e, m=m: e.dma_start(out=mk_sb[:, m * 512:(m + 1) * 512], in_=maskT[m, :, :]))

    for pr in range(4):
        p.dma("sp", [], ["qT"], lambda e, pr=pr: e.dma_start(out=qT[:, :], in_=qp[pr, :, :], transpose=True))
        p.dma("sp", [], ["kT"], lambda e, pr=pr: e.dma_start(out=kT[:, :], in_=kp[pr, :, :], transpose=True))
        for j in range(16):
            p.dma("pool", [], [("v_sb", j)],
                  lambda e, pr=pr, j=j: e.dma_start(
                      out=v_sb[:, :].rearrange("p (c d) -> p c d", d=128)[:, j * 4:(j + 1) * 4, :],
                      in_=vp[pr, j * 512:(j + 1) * 512, :].rearrange("(c p) d -> p c d", p=128)))
        for qs in range(16):
            nkc = 4 * qs + 4
            for kc in range(nkc):
                m = kc - 4 * qs
                ps, psk = pS2[kc % 2]
                p.emit("pe", ["qT", "kT"], [psk],
                       lambda e, ps=ps, kc=kc, qs=qs: e.matmul(ps[:, :512], kT[:, kc * 128:(kc + 1) * 128],
                                                               qT[:, qs * 512:(qs + 1) * 512], start=True, stop=True))
                pt = pT[kc % 2]
                ptk = ("pT", kc % 2)
                if m < 0:
                    p.emit("act", [psk, "zb"], [ptk],
                           lambda e, ps=ps, pt=pt: e.activation(out=pt[:, :], in_=ps[:, :512], func=AF.Exp,
                                                                bias=zb[:, :], scale=sc))
                else:
                    p.emit("dve", [psk, ("mk_sb", m)], ["dg"],
                           lambda e, ps=ps, m=m: e.scalar_tensor_tensor(out=dg[:, :], in0=ps[:, :512], scalar=sc,
                                                                        in1=mk_sb[:, m * 512:(m + 1) * 512],
                                                                        op0=ALU.mult, op1=ALU.add))
                    p.emit("act", ["dg", "zb"], [ptk],
                           lambda e, pt=pt: e.activation(out=pt[:, :], in_=dg[:, :], func=AF.Exp,
                                                         bias=zb[:, :], scale=1.0))
                first = kc == 0
                last = kc == nkc - 1
                for s in range(4):
                    po, pok = pO[s]
                    p.emit("pe", [ptk, ("v_sb", kc // 4)], [pok],
                           lambda e, po=po, pt=pt, s=s, kc=kc, first=first, last=last: e.matmul(
                               po[:, :128], pt[:, s * 128:(s + 1) * 128], v_sb[:, kc * 128:(kc + 1) * 128],
                               start=first, stop=last))
                pr_, prk = pR
                p.emit("pe", [ptk, "onc"], [prk],
                       lambda e, pr_=pr_, pt=pt, first=first, last=last: e.matmul(
                           pr_[0:1, :512], onc[:, :], pt[:, :], start=first, stop=last))
            p.emit("act", [pR[1]], ["rs_sb"],
                   lambda e: e.activation(out=rs_sb[:, :], in_=pR[0][0:1, :512], func=AF.Copy, bias=0.0, scale=1.0))
            pq, pqk = pQ
            for s in range(4):
                p.emit("pe", ["rs_sb", "onesf"], [pqk],
                       lambda e, pq=pq, s=s: e.matmul(pq[:, s:s + 1], rs_sb[0:1, s * 128:(s + 1) * 128],
                                                      onesf[0:1, 0:1], start=True, stop=True, skip_group_check=True))
            p.emit("dve", [pqk], ["rinv"], lambda e, pq=pq: e.reciprocal(out=rinv[:, :], in_=pq[:, 0:4]))
            for s in range(4):
                p.emit("dve", [pO[s][1], "rinv"], [("o_b", s)],
                       lambda e, s=s: e.tensor_scalar(out=o_b[:, s * 128:(s + 1) * 128], in0=pO[s][0][:, :128],
                                                      scalar1=rinv[:, s:s + 1], scalar2=None, op0=ALU.mult))
                p.dma("pool", [("o_b", s)], [],
                      lambda e, pr=pr, qs=qs, s=s: e.dma_start(out=op[pr, qs * 512 + s * 128:qs * 512 + (s + 1) * 128, :],
                                                               in_=o_b[:, s * 128:(s + 1) * 128]))




def build_a3(nc, p, stack):
    sb = stack.enter_context
    x = nc.dram_tensor("x", [NTOK, E], F32, kind="ExternalInput").ap()
    at = nc.dram_tensor("at", [NTOK, E], BF16, kind="ExternalInput").ap()
    Wo = nc.dram_tensor("Wo", [E, E], F32, kind="ExternalInput").ap()
    bo = nc.dram_tensor("bo", [E], F32, kind="ExternalInput").ap()
    gwT = nc.dram_tensor("gwT", [E, 8], F32, kind="ExternalInput").ap()
    ident = nc.dram_tensor("ident", [128, 128], BF16, kind="ExternalInput").ap()
    ones1 = nc.dram_tensor("ones1", [1, 128], BF16, kind="ExternalInput").ap()
    x1 = nc.dram_tensor("x1", [NTOK, E], F32, kind="ExternalOutput").ap()
    lnTo = nc.dram_tensor("lnTo", [16, 128, NTOK], BF16, kind="ExternalOutput").ap()
    cwo = nc.dram_tensor("cwo", [NTOK, 8], F32, kind="ExternalOutput").ap()

    x_t = [sb(nc.sbuf_tensor(f"x_t{i}", [128, E], F32)) for i in range(2)]
    at_t = [sb(nc.sbuf_tensor(f"at_t{i}", [128, E], BF16)) for i in range(2)]
    atT = sb(nc.sbuf_tensor("atT", [128, 16 * 128], BF16))
    w_sb = sb(nc.sbuf_tensor("w_sb", [128, 16 * E], BF16))
    gw_sb = sb(nc.sbuf_tensor("gw_sb", [128, 16 * 8], BF16))
    b_f = sb(nc.sbuf_tensor("b_f", [1, E], F32))
    b_b = sb(nc.sbuf_tensor("b_b", [1, E], BF16))
    on1 = sb(nc.sbuf_tensor("on1", [1, 128], BF16))
    id_b = sb(nc.sbuf_tensor("id_b", [128, 128], BF16))
    x1_sb = [sb(nc.sbuf_tensor(f"x1_sb{i}", [128, E], F32)) for i in range(2)]
    ln_b = [sb(nc.sbuf_tensor(f"ln_b{i}", [128, E], BF16)) for i in range(2)]
    lnT_t = sb(nc.sbuf_tensor("lnT_t", [128, 16 * 128], BF16))
    s8 = sb(nc.sbuf_tensor("s8", [128, 8], F32))
    m1 = sb(nc.sbuf_tensor("m1", [128, 8], F32))
    s2_ = sb(nc.sbuf_tensor("s2_", [128, 8], F32))
    sel = sb(nc.sbuf_tensor("sel", [128, 8], F32))
    w8 = sb(nc.sbuf_tensor("w8", [128, 8], F32))
    cw_sb = sb(nc.sbuf_tensor("cw_sb", [128, 8], F32))
    ngm = sb(nc.sbuf_tensor("ngm", [128, 1], F32))
    rsu = sb(nc.sbuf_tensor("rsu", [128, 1], F32))
    riv = sb(nc.sbuf_tensor("riv", [128, 1], F32))
    scr = {k: sb(nc.sbuf_tensor("scr_" + k, [128, d], F32)) for k, d in
           [("sq", E), ("s1", 1), ("s2", 1), ("nm", 1), ("sd", 1), ("rs", 1), ("eps", 1)]}
    pmm = [(sb(nc.psum_tensor(f"pmm{i}", [128, 512], F32)), f"pmm{i}") for i in range(2)]
    pbt = [(sb(nc.psum_tensor(f"pbt{i}", [128, 1024], BF16)), f"pbt{i}") for i in range(2)]
    pg = (sb(nc.psum_tensor("pg", [128, 512], F32)), "pg")

    p.emit("dve", [], ["eps"], lambda e: e.memset(scr["eps"][:, :], 1e-5))
    p.dma("sp", [], ["ident"], lambda e: e.dma_start(out=id_b[:, :], in_=ident[:, :]))
    p.dma("sp", [], ["on1"], lambda e: e.dma_start(out=on1[:, :], in_=ones1[:, :]))
    k_load_wcast(p, "pool", w_sb, "w_sb", Wo, 16, E, ndesc=4096)
    p.dma("pool", [], [("gw_sb", 0)],
          lambda e: e.dma_start(out=gw_sb[:, :].rearrange("p (kt n) -> p kt n", n=8),
                                in_=gwT.rearrange("(kt p) n -> p kt n", p=128)))
    p.dma("sp", [], ["b_f"], lambda e: e.dma_start(out=b_f[:, :], in_=bo.rearrange("(o e) -> o e", o=1)))
    p.emit("dve", ["b_f"], ["b_b"], lambda e: e.tensor_copy(out=b_b[:, :], in_=b_f[:, :]))

    for tt in range(NT):
        xk, xb = ("x_t", tt % 2), x_t[tt % 2]
        ak, ab = ("at_t", tt % 2), at_t[tt % 2]
        p.dma("sp", [], [xk], lambda e, tt=tt, xb=xb: e.dma_start(out=xb[:, :], in_=x[tt * 128:(tt + 1) * 128, :]))
        p.dma("sp", [], [ak], lambda e, tt=tt, ab=ab: e.dma_start(out=ab[:, :], in_=at[tt * 128:(tt + 1) * 128, :]))
        k_transpose16(p, ab, ak, atT, "atT", id_b, pbt, 16)
        x1b = x1_sb[tt % 2]
        x1k = ("x1_sb", tt % 2)
        for nch in range(4):
            ps, psk = pmm[nch % 2]
            p.emit("pe", ["b_b", "on1"], [psk],
                   lambda e, ps=ps, nch=nch: e.matmul(ps[:, :512], on1[0:1, :128], b_b[0:1, nch * 512:(nch + 1) * 512],
                                                      start=True, stop=False))
            for kt in range(16):
                p.emit("pe", [("atT", kt), ("w_sb", kt)], [psk],
                       lambda e, ps=ps, kt=kt, nch=nch: e.matmul(ps[:, :512], atT[:, kt * 128:(kt + 1) * 128],
                                                                 w_sb[:, kt * E + nch * 512:kt * E + (nch + 1) * 512],
                                                                 start=False, stop=(kt == 15)))
            p.emit("dve", [psk, xk], [(x1k, nch)],
                   lambda e, ps=ps, nch=nch, xb=xb, x1b=x1b: e.tensor_tensor(
                       out=x1b[:, nch * 512:(nch + 1) * 512], in0=ps[:, :512],
                       in1=xb[:, nch * 512:(nch + 1) * 512], op=ALU.add))
        x1ks = [(x1k, n) for n in range(4)]
        p.dma("pool", x1ks, [], lambda e, tt=tt, x1b=x1b: e.dma_start(out=x1[tt * 128:(tt + 1) * 128, :], in_=x1b[:, :]))
        lb, lk = ln_b[tt % 2], ("ln_b", tt % 2)
        k_ln(p, x1ks[0], x1b[:, :], lb[:, :], lk, scr, E, extra_reads=x1ks[1:])
        k_transpose16(p, lb, lk, lnT_t, "lnT_t", id_b, pbt, 16)
        for kt in range(16):
            p.dma("pool", [("lnT_t", kt)], [],
                  lambda e, tt=tt, kt=kt: e.dma_start(out=lnTo[kt, :, tt * 128:(tt + 1) * 128],
                                                      in_=lnT_t[:, kt * 128:(kt + 1) * 128]))
        ps, psk = pg
        for kt in range(16):
            p.emit("pe", [("lnT_t", kt), ("gw_sb", 0)], [psk],
                   lambda e, ps=ps, kt=kt: e.matmul(ps[:, :8], lnT_t[:, kt * 128:(kt + 1) * 128],
                                                    gw_sb[:, kt * 8:(kt + 1) * 8], start=(kt == 0), stop=(kt == 15)))
        p.emit("dve", [psk], ["ngm"], lambda e, ps=ps: e.reduce_max(ngm[:, :], ps[:, 0:8], axis=AX.X, negate=True))
        p.emit("act", [psk, "ngm"], ["s8"],
               lambda e, ps=ps: e.activation(out=s8[:, :], in_=ps[:, 0:8], func=AF.Exp, bias=ngm[:, :], scale=1.0))
        p.emit("dve", ["s8"], ["m1"],
               lambda e: e.tensor_scalar(out=m1[:, :], in0=s8[:, :], scalar1=1.0, scalar2=None, op0=ALU.is_ge))
        p.emit("dve", ["s8", "m1"], ["s2_"],
               lambda e: e.scalar_tensor_tensor(out=s2_[:, :], in0=m1[:, :], scalar=-2.0, in1=s8[:, :],
                                                op0=ALU.mult, op1=ALU.add))
        p.emit("dve", ["s2_", "s8"], ["ngm"], lambda e: e.reduce_max(ngm[:, :], s2_[:, :], axis=AX.X, negate=True))
        p.emit("dve", ["s8", "ngm"], ["sel"],
               lambda e: e.tensor_scalar(out=sel[:, :], in0=s8[:, :], scalar1=ngm[:, :], scalar2=0.0,
                                         op0=ALU.add, op1=ALU.is_ge))
        p.emit("dve", ["s8", "sel"], ["w8"],
               lambda e: e.tensor_tensor(out=w8[:, :], in0=s8[:, :], in1=sel[:, :], op=ALU.mult))
        p.emit("dve", ["w8"], ["rsu"], lambda e: e.reduce_sum(rsu[:, :], w8[:, :], axis=AX.X))
        p.emit("dve", ["rsu"], ["rsu"],
               lambda e: e.tensor_scalar(out=rsu[:, :], in0=rsu[:, :], scalar1=1e-20, scalar2=None, op0=ALU.add))
        p.emit("dve", ["rsu"], ["riv"], lambda e: e.reciprocal(out=riv[:, :], in_=rsu[:, :]))
        p.emit("dve", ["w8", "riv"], ["cw_sb"],
               lambda e: e.tensor_scalar(out=cw_sb[:, :], in0=w8[:, :], scalar1=riv[:, :], scalar2=None, op0=ALU.mult))
        p.dma("pool", ["cw_sb"], [],
              lambda e, tt=tt: e.dma_start(out=cwo[tt * 128:(tt + 1) * 128, :], in_=cw_sb[:, :]))



def build_b1(nc, p, stack):
    sb = stack.enter_context
    I = 1408
    w1 = nc.dram_tensor("w1", [E, 2 * I], F32, kind="ExternalInput").ap()
    w2 = nc.dram_tensor("w2", [I, E], F32, kind="ExternalInput").ap()
    lnTd = nc.dram_tensor("lnTd", [16, 128, T], BF16, kind="ExternalInput").ap()
    cwc = nc.dram_tensor("cwc", [T], F32, kind="ExternalInput").ap()
    yo = nc.dram_tensor("yo", [T, E], BF16, kind="ExternalOutput").ap()

    w1_sb = sb(nc.sbuf_tensor("w1_sb", [128, 16 * 2 * I], BF16))
    w2_sb = sb(nc.sbuf_tensor("w2_sb", [128, 11 * E], BF16))
    lnT_g = [sb(nc.sbuf_tensor(f"lnT_g{i}", [128, 16 * 512], BF16)) for i in range(2)]
    cw_sb = [sb(nc.sbuf_tensor(f"cw_sb{i}", [128, 4], F32)) for i in range(2)]
    g_tmp = sb(nc.sbuf_tensor("g_tmp", [128, 512], BF16))
    fuseT = sb(nc.sbuf_tensor("fuseT", [128, 11 * 512], BF16))
    y_sb = [sb(nc.sbuf_tensor(f"y_sb{i}", [128, E], BF16)) for i in range(2)]
    zb = sb(nc.sbuf_tensor("zb", [128, 1], F32))
    pA = [(sb(nc.psum_tensor(f"pA{i}", [128, 512], F32)), f"pA{i}") for i in range(4)]

    p.emit("dve", [], ["zb"], lambda e: e.memset(zb[:, :], 0.0))
    k_load_wcast(p, "pool", w1_sb, "w1_sb", w1, 16, 2 * I, ndesc=5632)
    k_load_wcast(p, "pool", w2_sb, "w2_sb", w2, 11, E, ndesc=4096)

    NG = T // 512
    for g in range(NG):
        gb = g % 2
        lg, cwb = lnT_g[gb], cw_sb[gb]
        for kt in range(16):
            p.dma("sp", [], [("lnT_g", gb, kt)],
                  lambda e, g=g, kt=kt, lg=lg: e.dma_start(out=lg[:, kt * 512:(kt + 1) * 512],
                                                           in_=lnTd[kt, :, g * 512:(g + 1) * 512]))
        for s in range(4):
            p.dma("sp", [], [("cw_sb", gb, s)],
                  lambda e, g=g, s=s, cwb=cwb: e.dma_start(out=cwb[:, s:s + 1],
                                                           in_=cwc[g * 512 + s * 128:g * 512 + (s + 1) * 128].rearrange("(p o) -> p o", o=1)))
        lgks = [("lnT_g", gb, kt) for kt in range(16)]
        for it in range(11):
            psg, psgk = pA[(2 * it) % 4]
            psu, psuk = pA[(2 * it + 1) % 4]
            for kt in range(16):
                p.emit("pe", [("w1_sb", kt), lgks[kt]], [psgk],
                       lambda e, psg=psg, kt=kt, it=it, lg=lg: e.matmul(
                           psg[:, :512], w1_sb[:, kt * 2 * I + it * 128:kt * 2 * I + (it + 1) * 128],
                           lg[:, kt * 512:(kt + 1) * 512], start=(kt == 0), stop=(kt == 15)))
            p.emit("act", [psgk, "zb"], ["g_tmp"],
                   lambda e, psg=psg: e.activation(out=g_tmp[:, :], in_=psg[:, :512], func=AF.Silu,
                                                   bias=zb[:, :], scale=1.0))
            for kt in range(16):
                p.emit("pe", [("w1_sb", kt), lgks[kt]], [psuk],
                       lambda e, psu=psu, kt=kt, it=it, lg=lg: e.matmul(
                           psu[:, :512], w1_sb[:, kt * 2 * I + I + it * 128:kt * 2 * I + I + (it + 1) * 128],
                           lg[:, kt * 512:(kt + 1) * 512], start=(kt == 0), stop=(kt == 15)))
            p.emit("dve", [psuk, "g_tmp"], [("fuseT", it)],
                   lambda e, psu=psu, it=it: e.tensor_tensor(out=fuseT[:, it * 512:(it + 1) * 512],
                                                             in0=psu[:, :512], in1=g_tmp[:, :], op=ALU.mult))
        yb = y_sb[gb]
        for s in range(4):
            for nch in range(4):
                ps, psk = pA[nch % 4]
                for it in range(11):
                    p.emit("pe", [("fuseT", it), ("w2_sb", it)], [psk],
                           lambda e, ps=ps, it=it, s=s, nch=nch: e.matmul(
                               ps[:, :512], fuseT[:, it * 512 + s * 128:it * 512 + (s + 1) * 128],
                               w2_sb[:, it * E + nch * 512:it * E + (nch + 1) * 512],
                               start=(it == 0), stop=(it == 10)))
                p.emit("dve", [psk, ("cw_sb", gb, s)], [("y_sb", gb, nch)],
                       lambda e, ps=ps, nch=nch, s=s, yb=yb, cwb=cwb: e.tensor_scalar(
                           out=yb[:, nch * 512:(nch + 1) * 512], in0=ps[:, :512],
                           scalar1=cwb[:, s:s + 1], scalar2=None, op0=ALU.mult))
            p.dma("pool", [("y_sb", gb, n) for n in range(4)], [],
                  lambda e, g=g, s=s, yb=yb: e.dma_start(out=yo[g * 512 + s * 128:g * 512 + (s + 1) * 128, :],
                                                         in_=yb[:, :]))



def build_b2(nc, p, stack):
    sb = stack.enter_context
    I = 1408
    x1 = nc.dram_tensor("x1", [NTOK, E], F32, kind="ExternalInput").ap()
    lnTd = nc.dram_tensor("lnTd", [16, 128, NTOK], BF16, kind="ExternalInput").ap()
    ye = nc.dram_tensor("ye", [8, NTOK, E], BF16, kind="ExternalInput").ap()
    sw1T = nc.dram_tensor("sw1T", [E, I], F32, kind="ExternalInput").ap()
    sw2T = nc.dram_tensor("sw2T", [E, I], F32, kind="ExternalInput").ap()
    sw3T = nc.dram_tensor("sw3T", [I, E], F32, kind="ExternalInput").ap()
    ident = nc.dram_tensor("ident", [128, 128], BF16, kind="ExternalInput").ap()
    x2 = nc.dram_tensor("x2", [NTOK, E], F32, kind="ExternalOutput").ap()

    w1_sb = sb(nc.sbuf_tensor("w1_sb", [128, 16 * I], BF16))
    w2_sb = sb(nc.sbuf_tensor("w2_sb", [128, 16 * I], BF16))
    w3_sb = sb(nc.sbuf_tensor("w3_sb", [128, 11 * E], BF16))
    lnT_g = sb(nc.sbuf_tensor("lnT_g", [128, 16 * 512], BF16))
    g_tmp = sb(nc.sbuf_tensor("g_tmp", [128, 512], BF16))
    fuseT = sb(nc.sbuf_tensor("fuseT", [128, 11 * 512], BF16))
    x1_sb = sb(nc.sbuf_tensor("x1_sb", [128, E], F32))
    ye_sb = [sb(nc.sbuf_tensor(f"ye_sb{i}", [128, 8 * 512], BF16)) for i in range(2)]
    t1_sb = sb(nc.sbuf_tensor("t1_sb", [128, E], F32))
    id_b = sb(nc.sbuf_tensor("id_b", [128, 128], BF16))
    zb = sb(nc.sbuf_tensor("zb", [128, 1], F32))
    pA = [(sb(nc.psum_tensor(f"pA{i}", [128, 512], F32)), f"pA{i}") for i in range(4)]

    p.emit("dve", [], ["zb"], lambda e: e.memset(zb[:, :], 0.0))
    p.dma("sp", [], ["ident"], lambda e: e.dma_start(out=id_b[:, :], in_=ident[:, :]))
    k_load_wcast(p, "pool", w1_sb, "w1_sb", sw1T, 16, I, ndesc=2944)
    k_load_wcast(p, "pool", w2_sb, "w2_sb", sw2T, 16, I, ndesc=2944)
    k_load_wcast(p, "pool", w3_sb, "w3_sb", sw3T, 11, E, ndesc=4096)

    for g in range(NTOK // 512):
        for kt in range(16):
            p.dma("sp", [], [("lnT_g", kt)],
                  lambda e, g=g, kt=kt: e.dma_start(out=lnT_g[:, kt * 512:(kt + 1) * 512],
                                                    in_=lnTd[kt, :, g * 512:(g + 1) * 512]))
        for it in range(11):
            psg, psgk = pA[(2 * it) % 4]
            psu, psuk = pA[(2 * it + 1) % 4]
            for kt in range(16):
                p.emit("pe", [("w1_sb", kt), ("lnT_g", kt)], [psgk],
                       lambda e, psg=psg, kt=kt, it=it: e.matmul(
                           psg[:, :512], w1_sb[:, kt * I + it * 128:kt * I + (it + 1) * 128],
                           lnT_g[:, kt * 512:(kt + 1) * 512], start=(kt == 0), stop=(kt == 15)))
            p.emit("act", [psgk, "zb"], ["g_tmp"],
                   lambda e, psg=psg: e.activation(out=g_tmp[:, :], in_=psg[:, :512], func=AF.Silu,
                                                   bias=zb[:, :], scale=1.0))
            for kt in range(16):
                p.emit("pe", [("w2_sb", kt), ("lnT_g", kt)], [psuk],
                       lambda e, psu=psu, kt=kt, it=it: e.matmul(
                           psu[:, :512], w2_sb[:, kt * I + it * 128:kt * I + (it + 1) * 128],
                           lnT_g[:, kt * 512:(kt + 1) * 512], start=(kt == 0), stop=(kt == 15)))
            p.emit("dve", [psuk, "g_tmp"], [("fuseT", it)],
                   lambda e, psu=psu, it=it: e.tensor_tensor(out=fuseT[:, it * 512:(it + 1) * 512],
                                                             in0=psu[:, :512], in1=g_tmp[:, :], op=ALU.mult))
        for s in range(4):
            row0 = g * 512 + s * 128
            p.dma("sp", [], ["x1_sb"], lambda e, row0=row0: e.dma_start(out=x1_sb[:, :], in_=x1[row0:row0 + 128, :]))
            for nch in range(4):
                yb = ye_sb[nch % 2]
                ybp = nch % 2
                for ex in range(8):
                    p.dma("pool", [], [("ye_sb", ybp, ex)],
                          lambda e, ex=ex, row0=row0, nch=nch, yb=yb: e.dma_start(
                              out=yb[:, ex * 512:(ex + 1) * 512],
                              in_=ye[ex, row0:row0 + 128, nch * 512:(nch + 1) * 512]))
                ps, psk = pA[nch % 4]
                for it in range(11):
                    p.emit("pe", [("fuseT", it), ("w3_sb", it)], [psk],
                           lambda e, ps=ps, it=it, s=s, nch=nch: e.matmul(
                               ps[:, :512], fuseT[:, it * 512 + s * 128:it * 512 + (s + 1) * 128],
                               w3_sb[:, it * E + nch * 512:it * E + (nch + 1) * 512],
                               start=(it == 0), stop=False))
                for ex in range(8):
                    p.emit("pe", [("ye_sb", ybp, ex), "ident"], [psk],
                           lambda e, ps=ps, ex=ex, yb=yb: e.matmul(
                               ps[:, :512], id_b[:, :], yb[:, ex * 512:(ex + 1) * 512],
                               start=False, stop=(ex == 7)))
                p.emit("dve", [psk, "x1_sb"], [("t1_sb", nch)],
                       lambda e, ps=ps, nch=nch: e.tensor_tensor(out=t1_sb[:, nch * 512:(nch + 1) * 512],
                                                                 in0=ps[:, :512],
                                                                 in1=x1_sb[:, nch * 512:(nch + 1) * 512], op=ALU.add))
            p.dma("pool", [("t1_sb", n) for n in range(4)], [],
                  lambda e, row0=row0: e.dma_start(out=x2[row0:row0 + 128, :], in_=t1_sb[:, :]))



def build_c1(nc, p, stack):
    sb = stack.enter_context
    O = 6144
    x = nc.dram_tensor("x", [NTOK, E], F32, kind="ExternalInput").ap()
    wT = nc.dram_tensor("wT", [E, O], F32, kind="ExternalInput").ap()
    ident = nc.dram_tensor("ident", [128, 128], BF16, kind="ExternalInput").ap()
    qkv = nc.dram_tensor("qkv", [NTOK, O], BF16, kind="ExternalOutput").ap()
    lnTo = nc.dram_tensor("lnTo", [16, 128, NTOK], BF16, kind="ExternalOutput").ap()

    x_t = [sb(nc.sbuf_tensor(f"x_t{i}", [128, E], F32)) for i in range(2)]
    ln_b = [sb(nc.sbuf_tensor(f"ln_b{i}", [128, E], BF16)) for i in range(2)]
    lnT = sb(nc.sbuf_tensor("lnT", [128, NT * 16 * 128], BF16))
    w_c = [sb(nc.sbuf_tensor(f"w_c{i}", [128, 16 * 512], BF16)) for i in range(2)]
    id_b = sb(nc.sbuf_tensor("id_b", [128, 128], BF16))
    o_b = sb(nc.sbuf_tensor("o_b", [128, 512], BF16))
    zb = sb(nc.sbuf_tensor("zb", [128, 1], F32))
    scr = {k: sb(nc.sbuf_tensor("scr_" + k, [128, d], F32)) for k, d in
           [("sq", E), ("s1", 1), ("s2", 1), ("nm", 1), ("sd", 1), ("rs", 1), ("eps", 1)]}
    pmm = [(sb(nc.psum_tensor(f"pmm{i}", [128, 512], F32)), f"pmm{i}") for i in range(2)]
    pbt = [(sb(nc.psum_tensor(f"pbt{i}", [128, 1024], BF16)), f"pbt{i}") for i in range(2)]

    p.emit("dve", [], ["eps"], lambda e: e.memset(scr["eps"][:, :], 1e-5))
    p.emit("dve", [], ["zb"], lambda e: e.memset(zb[:, :], 0.0))
    p.dma("sp", [], ["ident"], lambda e: e.dma_start(out=id_b[:, :], in_=ident[:, :]))

    for tt in range(NT):
        xk, xb = ("x_t", tt % 2), x_t[tt % 2]
        p.dma("sp", [], [xk], lambda e, tt=tt, xb=xb: e.dma_start(out=xb[:, :], in_=x[tt * 128:(tt + 1) * 128, :]))
        lb, lk = ln_b[tt % 2], ("ln_b", tt % 2)
        k_ln(p, xk, xb[:, :], lb[:, :], lk, scr, E)
        if A1_DEBUG:
            p.dma("pool", [lk], [], lambda e, tt=tt, lb=lb: e.dma_start(out=lno[tt * 128:(tt + 1) * 128, :], in_=lb[:, :]))
            for ci, (tile, kk) in enumerate([(scr["s1"], ("lnst", 0)), (scr["nm"], ("lnst", 1)),
                                             (scr["s2"], ("lnst", 3)), (scr["sd"], ("lnst", 4)),
                                             (scr["rs"], ("lnst", 5))]):
                p.emit("dve", [kk], [("stt_sb", ci)],
                       lambda e, ci=ci, tile=tile: e.tensor_copy(out=stt_sb[:, ci:ci + 1], in_=tile[:, :]))
            p.dma("pool", [("stt_sb", ci) for ci in range(5)], [],
                  lambda e, tt=tt: e.dma_start(out=stto[tt * 128:(tt + 1) * 128, :], in_=stt_sb[:, :]))
        k_transpose16(p, lb, lk, lnT, "lnT", id_b, pbt, 16, dst_off=tt * 16)
        if A1_DEBUG:
            for kt in range(16):
                p.dma("pool", [("lnT", tt * 16 + kt)], [],
                      lambda e, tt=tt, kt=kt: e.dma_start(out=lnTo[:, (tt * 16 + kt) * 128:(tt * 16 + kt + 1) * 128],
                                                          in_=lnT[:, (tt * 16 + kt) * 128:(tt * 16 + kt + 1) * 128]))
        for kt in range(16):
            p.dma("pool", [("lnT", tt * 16 + kt)], [],
                  lambda e, tt=tt, kt=kt: e.dma_start(out=lnTo[kt, :, tt * 128:(tt + 1) * 128],
                                                      in_=lnT[:, (tt * 16 + kt) * 128:(tt * 16 + kt + 1) * 128]))
    for nch in range(12):
        wb, wbk = w_c[nch % 2], nch % 2
        for kt in range(16):
            p.dma("pool", [], [("w_c", wbk, kt)],
                  lambda e, nch=nch, kt=kt, wb=wb: e.dma_start(out=wb[:, kt * 512:(kt + 1) * 512],
                                                               in_=wT[kt * 128:(kt + 1) * 128, nch * 512:(nch + 1) * 512]))
        for tt in range(NT):
            ps, psk = pmm[tt % 2]
            for kt in range(16):
                p.emit("pe", [("lnT", tt * 16 + kt), ("w_c", wbk, kt)], [psk],
                       lambda e, ps=ps, tt=tt, kt=kt, wb=wb: e.matmul(
                           ps[:, :512], lnT[:, (tt * 16 + kt) * 128:(tt * 16 + kt + 1) * 128],
                           wb[:, kt * 512:(kt + 1) * 512], start=(kt == 0), stop=(kt == 15)))
            p.emit("act", [psk, "zb"], ["o_b"],
                   lambda e, ps=ps: e.activation(out=o_b[:, :], in_=ps[:, :512], func=AF.Silu, bias=zb[:, :], scale=1.0))
            p.dma("pool", ["o_b"], [],
                  lambda e, tt=tt, nch=nch: e.dma_start(out=qkv[tt * 128:(tt + 1) * 128, nch * 512:(nch + 1) * 512],
                                                        in_=o_b[:, :]))



def build_c2(nc, p, stack):
    sb = stack.enter_context
    NP = 8
    qp = nc.dram_tensor("qp", [NP, S, 64], BF16, kind="ExternalInput").ap()
    kp = nc.dram_tensor("kp", [NP, S, 64], BF16, kind="ExternalInput").ap()
    vp = nc.dram_tensor("vp", [NP, S, 64], BF16, kind="ExternalInput").ap()
    ddT = nc.dram_tensor("ddT", [NP, 2, 128, 256], F32, kind="ExternalInput").ap()
    qd = nc.dram_tensor("qd", [NP, 128, 2], F32, kind="ExternalInput").ap()
    kd = nc.dram_tensor("kd", [NP, 128, 2], F32, kind="ExternalInput").ap()
    bdc = nc.dram_tensor("bdc", [NP, 64], F32, kind="ExternalInput").ap()
    ident = nc.dram_tensor("ident", [128, 128], BF16, kind="ExternalInput").ap()
    op = nc.dram_tensor("op", [NP, S, 64], BF16, kind="ExternalOutput").ap()

    q_sb = [sb(nc.sbuf_tensor(f"q_sb{i}", [128, 128], BF16)) for i in range(2)]
    k_sb = [sb(nc.sbuf_tensor(f"k_sb{i}", [128, 128], BF16)) for i in range(2)]
    v_sb = [sb(nc.sbuf_tensor(f"v_sb{i}", [128, 128], BF16)) for i in range(2)]
    kdk = sb(nc.sbuf_tensor("kdk", [128, 128], BF16))
    qT_sb = sb(nc.sbuf_tensor("qT_sb", [64, 256], BF16))
    kT_sb = sb(nc.sbuf_tensor("kT_sb", [64, 256], BF16))
    awT = sb(nc.sbuf_tensor("awT", [128, 2 * 256], BF16))
    dd_sb = sb(nc.sbuf_tensor("dd_sb", [128, 2 * 256], F32))
    qd_sb = sb(nc.sbuf_tensor("qd_sb", [128, 2], F32))
    kd_sb = sb(nc.sbuf_tensor("kd_sb", [128, 2], F32))
    bd_sb = sb(nc.sbuf_tensor("bd_sb", [64, 1], F32))
    st_f = sb(nc.sbuf_tensor("st_f", [64, 64], F32))
    st_b = sb(nc.sbuf_tensor("st_b", [64, 64], BF16))
    y_f = sb(nc.sbuf_tensor("y_f", [128, 2 * 64], F32))
    y_o = sb(nc.sbuf_tensor("y_o", [128, 2 * 64], BF16))
    id_b = sb(nc.sbuf_tensor("id_b", [128, 128], BF16))
    pT = [(sb(nc.psum_tensor(f"pT{i}", [128, 1024], BF16)), f"pT{i}") for i in range(2)]
    pA = (sb(nc.psum_tensor("pA", [128, 512], F32)), "pA")
    pN = (sb(nc.psum_tensor("pN", [128, 512], F32)), "pN")
    pI = [(sb(nc.psum_tensor(f"pI{i}", [128, 512], F32)), f"pI{i}") for i in range(2)]
    pU = (sb(nc.psum_tensor("pU", [128, 512], F32)), "pU")

    p.dma("sp", [], ["ident"], lambda e: e.dma_start(out=id_b[:, :], in_=ident[:, :]))

    def ldblk(pr, blk):
        pb = blk % 2
        r0 = blk * 256
        for qn, dst, dk, srcT in [("sp", q_sb[pb], ("q_sb", pb), qp), ("act", k_sb[pb], ("k_sb", pb), kp),
                                  ("pool", v_sb[pb], ("v_sb", pb), vp)]:
            p.dma(qn, [], [dk],
                  lambda e, srcT=srcT, dst=dst, pr=pr, r0=r0: e.dma_start(
                      out=dst[:, :].rearrange("p (c d) -> p c d", d=64),
                      in_=srcT[pr, r0:r0 + 256, :].rearrange("(c p) d -> p c d", p=128)))

    for pr in range(NP):
        p.dma("sp", [], ["dd0"], lambda e, pr=pr: e.dma_start(out=dd_sb[:, 0:256], in_=ddT[pr, 0, :, :]))
        p.dma("sp", [], ["dd1"], lambda e, pr=pr: e.dma_start(out=dd_sb[:, 256:512], in_=ddT[pr, 1, :, :]))
        p.dma("sp", [], ["qd_sb"], lambda e, pr=pr: e.dma_start(out=qd_sb[:, :], in_=qd[pr, :, :]))
        p.dma("sp", [], ["kd_sb"], lambda e, pr=pr: e.dma_start(out=kd_sb[:, :], in_=kd[pr, :, :]))
        p.dma("sp", [], ["bd_sb"], lambda e, pr=pr: e.dma_start(out=bd_sb[:, :], in_=bdc[pr, :].rearrange("(p o) -> p o", o=1)))
        p.emit("dve", [], ["st_f"], lambda e: e.memset(st_f[:, :], 0.0))
        p.emit("dve", ["st_f"], ["st_b"], lambda e: e.tensor_copy(out=st_b[:, :], in_=st_f[:, :]))
        ldblk(pr, 0)
        for blk in range(32):
            pb = blk % 2
            qs_, ks_, vs_ = q_sb[pb], k_sb[pb], v_sb[pb]
            qk, kk, vk = ("q_sb", pb), ("k_sb", pb), ("v_sb", pb)
            r0 = blk * 256
            if blk < 31:
                ldblk(pr, blk + 1)
            for jc in range(2):
                p.emit("dve", [kk, "kd_sb"], [("kdk", jc)],
                       lambda e, jc=jc, ks_=ks_: e.tensor_scalar(out=kdk[:, jc * 64:(jc + 1) * 64],
                                                                 in0=ks_[:, jc * 64:(jc + 1) * 64],
                                                                 scalar1=kd_sb[:, jc:jc + 1], scalar2=None, op0=ALU.mult))
            pt0, pt0k = pT[0]
            pt1, pt1k = pT[1]
            p.emit("pe", [qk, "ident"], [pt0k],
                   lambda e, qs_=qs_: e.transpose(pt0[:64, :128], qs_[:, 0:64], id_b[:, :]))
            p.emit("pe", [kk, "ident"], [pt1k],
                   lambda e, ks_=ks_: e.transpose(pt1[:64, :128], ks_[:, 0:64], id_b[:, :]))
            p.emit("dve", [pt0k], [("qT_sb", 0)],
                   lambda e: e.tensor_copy(out=qT_sb[:, 0:128], in_=pt0[:64, :128]))
            p.emit("pe", [qk, "ident"], [pt0k],
                   lambda e, qs_=qs_: e.transpose(pt0[:64, :128], qs_[:, 64:128], id_b[:, :]))
            p.emit("dve", [pt1k], [("kT_sb", 0)],
                   lambda e: e.tensor_copy(out=kT_sb[:, 0:128], in_=pt1[:64, :128]))
            p.emit("pe", [kk, "ident"], [pt1k],
                   lambda e, ks_=ks_: e.transpose(pt1[:64, :128], ks_[:, 64:128], id_b[:, :]))
            p.emit("dve", [pt0k], [("qT_sb", 1)],
                   lambda e: e.tensor_copy(out=qT_sb[:, 128:256], in_=pt0[:64, :128]))
            p.emit("dve", [pt1k], [("kT_sb", 1)],
                   lambda e: e.tensor_copy(out=kT_sb[:, 128:256], in_=pt1[:64, :128]))
            for ic in range(2):
                ps, psk = pN
                p.emit("pe", [("qT_sb", ic), "st_b"], [psk],
                       lambda e, ps=ps, ic=ic: e.matmul(ps[:, :64], qT_sb[:, ic * 128:(ic + 1) * 128],
                                                        st_b[:, :], start=True, stop=True))
                p.emit("dve", [psk, "qd_sb"], [("y_f", ic)],
                       lambda e, ps=ps, ic=ic: e.tensor_scalar(out=y_f[:, ic * 64:(ic + 1) * 64], in0=ps[:, 0:64],
                                                               scalar1=qd_sb[:, ic:ic + 1], scalar2=None, op0=ALU.mult))
            for jc in range(2):
                ps, psk = pA
                p.emit("pe", [("kT_sb", jc), ("qT_sb", 0), ("qT_sb", 1)], [psk],
                       lambda e, ps=ps, jc=jc: e.matmul(ps[:, :256], kT_sb[:, jc * 128:(jc + 1) * 128],
                                                        qT_sb[:64, :256], start=True, stop=True))
                p.emit("dve", [psk, "dd0", "dd1"], [("awT", jc)],
                       lambda e, ps=ps, jc=jc: e.tensor_tensor(out=awT[:, jc * 256:(jc + 1) * 256],
                                                               in0=ps[:, :256], in1=dd_sb[:, jc * 256:(jc + 1) * 256],
                                                               op=ALU.mult))
            for ic in range(2):
                ps, psk = pI[ic]
                for jc in range(2):
                    p.emit("pe", [("awT", jc), vk], [psk],
                           lambda e, ps=ps, ic=ic, jc=jc, vs_=vs_: e.matmul(ps[:, :64],
                                                                            awT[:, jc * 256 + ic * 128:jc * 256 + (ic + 1) * 128],
                                                                            vs_[:, jc * 64:(jc + 1) * 64],
                                                                            start=(jc == 0), stop=(jc == 1)))
                p.emit("dve", [psk, ("y_f", ic)], [("y_o", ic)],
                       lambda e, ps=ps, ic=ic: e.scalar_tensor_tensor(out=y_o[:, ic * 64:(ic + 1) * 64],
                                                                      in0=ps[:, 0:64], scalar=1.0,
                                                                      in1=y_f[:, ic * 64:(ic + 1) * 64],
                                                                      op0=ALU.mult, op1=ALU.add))
            p.dma("pool", [("y_o", 0), ("y_o", 1)], [],
                  lambda e, pr=pr, r0=r0: e.dma_start(out=op[pr, r0:r0 + 256, :].rearrange("(c p) d -> p c d", p=128),
                                                      in_=y_o[:, :].rearrange("p (c d) -> p c d", d=64)))
            ps, psk = pU
            for jc in range(2):
                p.emit("pe", [("kdk", jc), vk], [psk],
                       lambda e, ps=ps, jc=jc, vs_=vs_: e.matmul(ps[:64, :64], kdk[:, jc * 64:(jc + 1) * 64],
                                                                 vs_[:, jc * 64:(jc + 1) * 64],
                                                                 start=(jc == 0), stop=(jc == 1)))
            p.emit("dve", [psk, "bd_sb", "st_f"], ["st_f"],
                   lambda e, ps=ps: e.scalar_tensor_tensor(out=st_f[:, :], in0=st_f[:, :], scalar=bd_sb[:, :],
                                                           in1=ps[:64, 0:64], op0=ALU.mult, op1=ALU.add))
            p.emit("dve", ["st_f"], ["st_b"], lambda e: e.tensor_copy(out=st_b[:, :], in_=st_f[:, :]))



def build_c3(nc, p, stack):
    sb = stack.enter_context
    x = nc.dram_tensor("x", [NTOK, E], F32, kind="ExternalInput").ap()
    at = nc.dram_tensor("at", [NTOK, E], BF16, kind="ExternalInput").ap()
    lnTd = nc.dram_tensor("lnTd", [16, 128, NTOK], BF16, kind="ExternalInput").ap()
    nw = nc.dram_tensor("nw", [E], F32, kind="ExternalInput").ap()
    gwT = nc.dram_tensor("gwT", [E, E], F32, kind="ExternalInput").ap()
    owT = nc.dram_tensor("owT", [E, E], F32, kind="ExternalInput").ap()
    ident = nc.dram_tensor("ident", [128, 128], BF16, kind="ExternalInput").ap()
    ones1 = nc.dram_tensor("ones1", [1, 128], BF16, kind="ExternalInput").ap()
    x3 = nc.dram_tensor("x3", [NTOK, E], F32, kind="ExternalOutput").ap()

    x_t = sb(nc.sbuf_tensor("x_t", [128, E], F32))
    at_t = sb(nc.sbuf_tensor("at_t", [128, E], BF16))
    lnT_t = sb(nc.sbuf_tensor("lnT_t", [128, 16 * 128], BF16))
    gw_sb = sb(nc.sbuf_tensor("gw_sb", [128, 16 * E], BF16))
    ow_sb = sb(nc.sbuf_tensor("ow_sb", [128, 16 * E], BF16))
    nw_f = sb(nc.sbuf_tensor("nw_f", [1, E], F32))
    nw_b = sb(nc.sbuf_tensor("nw_b", [1, E], BF16))
    nwb = sb(nc.sbuf_tensor("nwb", [128, E], F32))
    on1 = sb(nc.sbuf_tensor("on1", [1, 128], BF16))
    id_b = sb(nc.sbuf_tensor("id_b", [128, 128], BF16))
    rn_sb = sb(nc.sbuf_tensor("rn_sb", [128, E], F32))
    g_tmp = sb(nc.sbuf_tensor("g_tmp", [128, 512], BF16))
    gr_sb = sb(nc.sbuf_tensor("gr_sb", [128, E], BF16))
    grT = sb(nc.sbuf_tensor("grT", [128, 16 * 128], BF16))
    x3_sb = sb(nc.sbuf_tensor("x3_sb", [128, E], F32))
    zb = sb(nc.sbuf_tensor("zb", [128, 1], F32))
    scr = {k: sb(nc.sbuf_tensor("scr_" + k, [128, d], F32)) for k, d in
           [("sq", E), ("s1", 1), ("sd", 1), ("rs", 1), ("eps", 1)]}
    pmm = [(sb(nc.psum_tensor(f"pmm{i}", [128, 512], F32)), f"pmm{i}") for i in range(2)]
    pbt = [(sb(nc.psum_tensor(f"pbt{i}", [128, 1024], BF16)), f"pbt{i}") for i in range(2)]

    p.emit("dve", [], ["eps"], lambda e: e.memset(scr["eps"][:, :], 1e-5))
    p.emit("dve", [], ["zb"], lambda e: e.memset(zb[:, :], 0.0))
    p.dma("sp", [], ["ident"], lambda e: e.dma_start(out=id_b[:, :], in_=ident[:, :]))
    p.dma("sp", [], ["on1"], lambda e: e.dma_start(out=on1[:, :], in_=ones1[:, :]))
    k_load_wcast(p, "pool", gw_sb, "gw_sb", gwT, 16, E, ndesc=4096)
    k_load_wcast(p, "pool", ow_sb, "ow_sb", owT, 16, E, ndesc=4096)
    p.dma("sp", [], ["nw_f"], lambda e: e.dma_start(out=nw_f[:, :], in_=nw.rearrange("(o e) -> o e", o=1)))
    p.emit("dve", ["nw_f"], ["nw_b"], lambda e: e.tensor_copy(out=nw_b[:, :], in_=nw_f[:, :]))
    for nch in range(4):
        ps, psk = pmm[nch % 2]
        p.emit("pe", ["nw_b", "on1"], [psk],
               lambda e, ps=ps, nch=nch: e.matmul(ps[:, :512], on1[0:1, :128], nw_b[0:1, nch * 512:(nch + 1) * 512],
                                                  start=True, stop=True))
        p.emit("act", [psk], [("nwb", nch)],
               lambda e, ps=ps, nch=nch: e.activation(out=nwb[:, nch * 512:(nch + 1) * 512], in_=ps[:, :512],
                                                      func=AF.Copy, bias=0.0, scale=1.0))

    for tt in range(NT):
        p.dma("sp", [], ["x_t"], lambda e, tt=tt: e.dma_start(out=x_t[:, :], in_=x[tt * 128:(tt + 1) * 128, :]))
        p.dma("sp", [], ["at_t"], lambda e, tt=tt: e.dma_start(out=at_t[:, :], in_=at[tt * 128:(tt + 1) * 128, :]))
        for kt in range(16):
            p.dma("sp", [], [("lnT_t", kt)],
                  lambda e, tt=tt, kt=kt: e.dma_start(out=lnT_t[:, kt * 128:(kt + 1) * 128],
                                                      in_=lnTd[kt, :, tt * 128:(tt + 1) * 128]))
        p.emit("act", ["at_t", "zb"], [("rms", 0)],
               lambda e: e.activation(out=scr["sq"][:, :], in_=at_t[:, :], func=AF.Square, bias=zb[:, :], scale=1.0))
        p.emit("dve", [("rms", 0)], [("rms", 1)], lambda e: e.reduce_sum(scr["s1"][:, :], scr["sq"][:, :], axis=AX.X))
        p.emit("act", [("rms", 1), "eps"], [("rms", 2)],
               lambda e: e.activation(out=scr["sd"][:, :], in_=scr["s1"][:, :], func=AF.Sqrt, bias=scr["eps"][:, :],
                                      scale=1.0 / E))
        p.emit("dve", [("rms", 2)], [("rms", 3)], lambda e: e.reciprocal(out=scr["rs"][:, :], in_=scr["sd"][:, :]))
        p.emit("dve", ["at_t", ("rms", 3)], ["rn_sb"],
               lambda e: e.tensor_scalar(out=rn_sb[:, :], in0=at_t[:, :], scalar1=scr["rs"][:, :], scalar2=None,
                                         op0=ALU.mult))
        p.emit("dve", ["rn_sb"] + [("nwb", n) for n in range(4)], ["rn_sb"],
               lambda e: e.tensor_tensor(out=rn_sb[:, :], in0=rn_sb[:, :], in1=nwb[:, :], op=ALU.mult))
        for nch in range(4):
            ps, psk = pmm[nch % 2]
            for kt in range(16):
                p.emit("pe", [("lnT_t", kt), ("gw_sb", kt)], [psk],
                       lambda e, ps=ps, kt=kt, nch=nch: e.matmul(ps[:, :512], lnT_t[:, kt * 128:(kt + 1) * 128],
                                                                 gw_sb[:, kt * E + nch * 512:kt * E + (nch + 1) * 512],
                                                                 start=(kt == 0), stop=(kt == 15)))
            p.emit("act", [psk, "zb"], ["g_tmp"],
                   lambda e, ps=ps: e.activation(out=g_tmp[:, :], in_=ps[:, :512], func=AF.Sigmoid, bias=zb[:, :],
                                                 scale=1.0))
            p.emit("dve", ["g_tmp", "rn_sb"], [("gr_sb", nch)],
                   lambda e, nch=nch: e.tensor_tensor(out=gr_sb[:, nch * 512:(nch + 1) * 512], in0=g_tmp[:, :],
                                                      in1=rn_sb[:, nch * 512:(nch + 1) * 512], op=ALU.mult))
        k_transpose16(p, gr_sb, [("gr_sb", i // 4) for i in range(16)], grT, "grT", id_b, pbt, 16)
        for nch in range(4):
            ps, psk = pmm[nch % 2]
            for kt in range(16):
                p.emit("pe", [("grT", kt), ("ow_sb", kt)], [psk],
                       lambda e, ps=ps, kt=kt, nch=nch: e.matmul(ps[:, :512], grT[:, kt * 128:(kt + 1) * 128],
                                                                 ow_sb[:, kt * E + nch * 512:kt * E + (nch + 1) * 512],
                                                                 start=(kt == 0), stop=(kt == 15)))
            p.emit("dve", [psk, "x_t"], [("x3_sb", nch)],
                   lambda e, ps=ps, nch=nch: e.tensor_tensor(out=x3_sb[:, nch * 512:(nch + 1) * 512], in0=ps[:, :512],
                                                             in1=x_t[:, nch * 512:(nch + 1) * 512], op=ALU.add))
        p.dma("pool", [("x3_sb", n) for n in range(4)], [],
              lambda e, tt=tt: e.dma_start(out=x3[tt * 128:(tt + 1) * 128, :], in_=x3_sb[:, :]))



def build_d1(nc, p, stack):
    sb = stack.enter_context
    O = 8192
    x = nc.dram_tensor("x", [NTOK, E], F32, kind="ExternalInput").ap()
    wg = nc.dram_tensor("wg", [E, O], I8, kind="ExternalInput").ap()
    wu = nc.dram_tensor("wu", [E, O], I8, kind="ExternalInput").ap()
    wd = nc.dram_tensor("wd", [O, E], I8, kind="ExternalInput").ap()
    sg = nc.dram_tensor("sg", [O], F32, kind="ExternalInput").ap()
    su = nc.dram_tensor("su", [O], F32, kind="ExternalInput").ap()
    sd = nc.dram_tensor("sd", [E], F32, kind="ExternalInput").ap()
    ident = nc.dram_tensor("ident", [128, 128], BF16, kind="ExternalInput").ap()
    ones1 = nc.dram_tensor("ones1", [1, 128], BF16, kind="ExternalInput").ap()
    x4 = nc.dram_tensor("x4", [NTOK, E], F32, kind="ExternalOutput").ap()

    x3_sb = sb(nc.sbuf_tensor("x3_sb", [128, 4 * E], F32))
    acc = sb(nc.sbuf_tensor("acc", [128, 4 * E], F32))
    lnT = sb(nc.sbuf_tensor("lnT", [128, 16 * 512], BF16))
    ln_b = sb(nc.sbuf_tensor("ln_b", [128, E], BF16))
    wg_c = [sb(nc.sbuf_tensor(f"wg_c{i}", [128, 16 * 512], BF16)) for i in range(2)]
    wu_c = [sb(nc.sbuf_tensor(f"wu_c{i}", [128, 16 * 512], BF16)) for i in range(2)]
    wd_c = [sb(nc.sbuf_tensor("wd_c0", [128, 4 * E], BF16))] * 2
    fuseT = sb(nc.sbuf_tensor("fuseT", [128, 4 * 512], BF16))
    g_tmp = sb(nc.sbuf_tensor("g_tmp", [128, 512], BF16))
    sg_sb = sb(nc.sbuf_tensor("sg_sb", [128, 64], F32))
    su_sb = sb(nc.sbuf_tensor("su_sb", [128, 64], F32))
    sd_f = sb(nc.sbuf_tensor("sd_f", [1, E], F32))
    sd_b = sb(nc.sbuf_tensor("sd_b", [1, E], BF16))
    sdb = sb(nc.sbuf_tensor("sdb", [128, E], F32))
    t2_sb = sb(nc.sbuf_tensor("t2_sb", [128, E], F32))
    on1 = sb(nc.sbuf_tensor("on1", [1, 128], BF16))
    id_b = sb(nc.sbuf_tensor("id_b", [128, 128], BF16))
    zb = sb(nc.sbuf_tensor("zb", [128, 1], F32))
    scr = {k: sb(nc.sbuf_tensor("scr_" + k, [128, d], F32)) for k, d in
           [("sq", E), ("s1", 1), ("s2", 1), ("nm", 1), ("sd", 1), ("rs", 1), ("eps", 1)]}
    pmm = [(sb(nc.psum_tensor(f"pmm{i}", [128, 512], F32)), f"pmm{i}") for i in range(4)]
    pbt = [(sb(nc.psum_tensor(f"pbt{i}", [128, 1024], BF16)), f"pbt{i}") for i in range(2)]

    p.emit("dve", [], ["eps"], lambda e: e.memset(scr["eps"][:, :], 1e-5))
    p.emit("dve", [], ["zb"], lambda e: e.memset(zb[:, :], 0.0))
    p.dma("sp", [], ["ident"], lambda e: e.dma_start(out=id_b[:, :], in_=ident[:, :]))
    p.dma("sp", [], ["on1"], lambda e: e.dma_start(out=on1[:, :], in_=ones1[:, :]))
    for ot in range(64):
        p.dma("sp", [], [("sg_sb", ot)],
              lambda e, ot=ot: e.dma_start(out=sg_sb[:, ot:ot + 1], in_=sg[ot * 128:(ot + 1) * 128].rearrange("(p o) -> p o", o=1)))
        p.dma("sp", [], [("su_sb", ot)],
              lambda e, ot=ot: e.dma_start(out=su_sb[:, ot:ot + 1], in_=su[ot * 128:(ot + 1) * 128].rearrange("(p o) -> p o", o=1)))
    p.dma("sp", [], ["sd_f"], lambda e: e.dma_start(out=sd_f[:, :], in_=sd.rearrange("(o e) -> o e", o=1)))
    p.emit("dve", ["sd_f"], ["sd_b"], lambda e: e.tensor_copy(out=sd_b[:, :], in_=sd_f[:, :]))
    for nch in range(4):
        ps, psk = pmm[nch % 2]
        p.emit("pe", ["sd_b", "on1"], [psk],
               lambda e, ps=ps, nch=nch: e.matmul(ps[:, :512], on1[0:1, :128], sd_b[0:1, nch * 512:(nch + 1) * 512],
                                                  start=True, stop=True))
        p.emit("act", [psk], [("sdb", nch)],
               lambda e, ps=ps, nch=nch: e.activation(out=sdb[:, nch * 512:(nch + 1) * 512], in_=ps[:, :512],
                                                      func=AF.Copy, bias=0.0, scale=1.0))

    for grp in range(4):
        for ttt in range(4):
            tt = grp * 4 + ttt
            p.dma("sp", [], [("x3_sb", ttt)],
                  lambda e, tt=tt, ttt=ttt: e.dma_start(out=x3_sb[:, ttt * E:(ttt + 1) * E],
                                                        in_=x[tt * 128:(tt + 1) * 128, :]))
            k_ln(p, ("x3_sb", ttt), x3_sb[:, ttt * E:(ttt + 1) * E], ln_b[:, :], "ln_b", scr, E)
            for kt in range(16):
                pb, pbk = pbt[kt % 2]
                p.emit("pe", ["ln_b", "ident"], [pbk],
                       lambda e, pb=pb, kt=kt: e.transpose(pb[:, :128], ln_b[:, kt * 128:(kt + 1) * 128], id_b[:, :]))
                p.emit("dve", [pbk], [("lnT", kt, ttt)],
                       lambda e, pb=pb, kt=kt, ttt=ttt: e.tensor_copy(
                           out=lnT[:, kt * 512 + ttt * 128:kt * 512 + (ttt + 1) * 128], in_=pb[:, :128]))
        for oc in range(16):
            cb = oc % 2
            wgc, wuc, wdc = wg_c[cb], wu_c[cb], wd_c[cb]
            for kt in range(16):
                p.dma("pool", [], [("wg_c", cb, kt)],
                      lambda e, oc=oc, kt=kt, wgc=wgc: e.dma_start(out=wgc[:, kt * 512:(kt + 1) * 512],
                                                                   in_=wg[kt * 128:(kt + 1) * 128, oc * 512:(oc + 1) * 512]))
                p.dma("pool", [], [("wu_c", cb, kt)],
                      lambda e, oc=oc, kt=kt, wuc=wuc: e.dma_start(out=wuc[:, kt * 512:(kt + 1) * 512],
                                                                   in_=wu[kt * 128:(kt + 1) * 128, oc * 512:(oc + 1) * 512]))
            for j in range(4):
                p.dma("pool", [], [("wd_c", 0, j)],
                      lambda e, oc=oc, j=j, wdc=wdc: e.dma_start(out=wdc[:, j * E:(j + 1) * E],
                                                                 in_=wd[oc * 512 + j * 128:oc * 512 + (j + 1) * 128, :]))
            for ot in range(4):
                ov = oc * 4 + ot
                psg, psgk = pmm[0]
                psu, psuk = pmm[1]
                for kt in range(16):
                    p.emit("pe", [("wg_c", cb, kt)] + [("lnT", kt, t3) for t3 in range(4)], [psgk],
                           lambda e, psg=psg, kt=kt, ot=ot, wgc=wgc: e.matmul(
                               psg[:, :512], wgc[:, kt * 512 + ot * 128:kt * 512 + (ot + 1) * 128],
                               lnT[:, kt * 512:(kt + 1) * 512], start=(kt == 0), stop=(kt == 15)))
                p.emit("act", [psgk, ("sg_sb", ov)], ["g_tmp"],
                       lambda e, psg=psg, ov=ov: e.activation(out=g_tmp[:, :], in_=psg[:, :512], func=AF.Silu,
                                                              bias=zb[:, :], scale=sg_sb[:, ov:ov + 1]))
                for kt in range(16):
                    p.emit("pe", [("wu_c", cb, kt)] + [("lnT", kt, t3) for t3 in range(4)], [psuk],
                           lambda e, psu=psu, kt=kt, ot=ot, wuc=wuc: e.matmul(
                               psu[:, :512], wuc[:, kt * 512 + ot * 128:kt * 512 + (ot + 1) * 128],
                               lnT[:, kt * 512:(kt + 1) * 512], start=(kt == 0), stop=(kt == 15)))
                p.emit("dve", [psuk, "g_tmp", ("su_sb", ov)], [("fuseT", ot)],
                       lambda e, psu=psu, ov=ov, ot=ot: e.scalar_tensor_tensor(
                           out=fuseT[:, ot * 512:(ot + 1) * 512], in0=psu[:, :512], scalar=su_sb[:, ov:ov + 1],
                           in1=g_tmp[:, :], op0=ALU.mult, op1=ALU.mult))
            for ttt in range(4):
                for nch in range(4):
                    ps, psk = pmm[2 + nch % 2]
                    for ot in range(4):
                        p.emit("pe", [("fuseT", ot), ("wd_c", 0, ot)], [psk],
                               lambda e, ps=ps, ot=ot, ttt=ttt, nch=nch, wdc=wdc: e.matmul(
                                   ps[:, :512], fuseT[:, ot * 512 + ttt * 128:ot * 512 + (ttt + 1) * 128],
                                   wdc[:, ot * E + nch * 512:ot * E + (nch + 1) * 512],
                                   start=(ot == 0), stop=(ot == 3)))
                    if oc == 0:
                        p.emit("act", [psk], [("acc", ttt, nch)],
                               lambda e, ps=ps, ttt=ttt, nch=nch: e.activation(
                                   out=acc[:, ttt * E + nch * 512:ttt * E + (nch + 1) * 512], in_=ps[:, :512],
                                   func=AF.Copy, bias=0.0, scale=1.0))
                    else:
                        p.emit("dve", [psk, ("acc", ttt, nch)], [("acc", ttt, nch)],
                               lambda e, ps=ps, ttt=ttt, nch=nch: e.tensor_tensor(
                                   out=acc[:, ttt * E + nch * 512:ttt * E + (nch + 1) * 512], in0=ps[:, :512],
                                   in1=acc[:, ttt * E + nch * 512:ttt * E + (nch + 1) * 512], op=ALU.add))
        for ttt in range(4):
            tt = grp * 4 + ttt
            p.emit("dve", [("acc", ttt, n) for n in range(4)] + [("sdb", n) for n in range(4)], ["t2_sb"],
                   lambda e, ttt=ttt: e.tensor_tensor(out=t2_sb[:, :], in0=acc[:, ttt * E:(ttt + 1) * E],
                                                      in1=sdb[:, :], op=ALU.mult))
            p.emit("dve", ["t2_sb", ("x3_sb", ttt)], ["t2_sb"],
                   lambda e, ttt=ttt: e.tensor_tensor(out=t2_sb[:, :], in0=t2_sb[:, :],
                                                      in1=x3_sb[:, ttt * E:(ttt + 1) * E], op=ALU.add))
            p.dma("pool", ["t2_sb"], [],
                  lambda e, tt=tt: e.dma_start(out=x4[tt * 128:(tt + 1) * 128, :], in_=t2_sb[:, :]))



def rope_tables():
    pos = np.arange(S, dtype=np.float32)[:, None]
    exps = (-np.arange(0, 128, 2, dtype=np.float32) / 128.0)
    thetas = np.repeat(np.power(np.float32(10000.0), exps), 2)
    ang = pos * thetas[None, :]
    cos = np.cos(ang).astype(np.float32)
    sin = np.sin(ang).astype(np.float32)
    return np.tile(cos, (1, 16)), np.tile(sin, (1, 16))


def decay_buffers():
    H, BS = 32, 256
    base = 1.0 / 2.0 ** (8.0 / H)
    rate = base ** (np.arange(H, dtype=np.float32) + 1.0)
    factor = 1.0 - 0.0 / (32 - 1 + 1e-5) + 1e-5
    slope = (rate * factor).astype(np.float32)[:, None, None]
    r = np.arange(BS, dtype=np.float32) + 1.0
    qd = np.exp(-slope * r[:, None]).astype(np.float32)
    kd = np.exp(-slope * (BS - r[:, None])).astype(np.float32)
    ddm = r[:, None] - r[None, :]
    ddm = slope * ddm[None, :, :][0]
    dd = np.where(ddm >= 0, -ddm, -np.inf)
    dd = np.exp(dd).astype(np.float32)
    bd = np.exp(-slope[:, 0, 0] * BS).astype(np.float32)
    qd_pc = qd[:, :, 0].reshape(H, 2, 128).transpose(0, 2, 1).copy()
    kd_pc = kd[:, :, 0].reshape(H, 2, 128).transpose(0, 2, 1).copy()
    ddT = dd.transpose(0, 2, 1).reshape(H, 2, 128, 256).copy()
    return qd_pc, kd_pc, ddT, bd


def masksT():
    mk = np.zeros((4, 128, 512), dtype=np.float32)
    kk = np.arange(128)[:, None]
    qq = np.arange(512)[None, :]
    for m in range(4):
        mk[m] = np.where(kk <= qq - m * 128, 0.0, -1e30)
    return mk



def kernel(**inp):
    ident = np.eye(128, dtype=NPBF16)
    ones1 = np.ones((1, 128), dtype=NPBF16)
    onesc = np.ones((128, 1), dtype=NPBF16)
    cosr, sinr = rope_tables()
    mk = masksT()

    x_flat = np.ascontiguousarray(inp["L0_attn_x"].reshape(T, E).astype(np.float32))

    def shard_rows(a):
        return [np.ascontiguousarray(a[c * NTOK:(c + 1) * NTOK]) for c in range(NCORE)]

    maps = []
    for c in range(NCORE):
        pos0 = (c * NTOK) % S
        maps.append({"x": shard_rows(x_flat)[c],
                     "Wq": inp["L0_attn_Wq"], "Wk": inp["L0_attn_Wk"], "Wv": inp["L0_attn_Wv"],
                     "bq": inp["L0_attn_bq"], "bk": inp["L0_attn_bk"], "bv": inp["L0_attn_bv"],
                     "cosr": np.ascontiguousarray(cosr[pos0:pos0 + NTOK]),
                     "sinr": np.ascontiguousarray(sinr[pos0:pos0 + NTOK]),
                     "ident": ident, "ones1": ones1})
    r = run_launch(build_a1, maps)
    q_all = np.concatenate([rr["qo"] for rr in r], axis=0).reshape(2, S, 16, 128)
    k_all = np.concatenate([rr["ko"] for rr in r], axis=0).reshape(2, S, 16, 128)
    v_all = np.concatenate([rr["vo"] for rr in r], axis=0).reshape(2, S, 16, 128)

    maps = []
    for c in range(NCORE):
        prs = [(idx // 16, idx % 16) for idx in range(c * 4, c * 4 + 4)]
        maps.append({"qp": np.ascontiguousarray(np.stack([q_all[b, :, h, :] for b, h in prs])),
                     "kp": np.ascontiguousarray(np.stack([k_all[b, :, h, :] for b, h in prs])),
                     "vp": np.ascontiguousarray(np.stack([v_all[b, :, h, :] for b, h in prs])),
                     "maskT": mk, "onesc": onesc, "ones1": ones1})
    r = run_launch(build_a2, maps)
    at4 = np.empty((2, S, 16, 128), dtype=NPBF16)
    for c in range(NCORE):
        for j, idx in enumerate(range(c * 4, c * 4 + 4)):
            at4[idx // 16, :, idx % 16, :] = r[c]["op"][j]
    attn_flat = np.ascontiguousarray(at4.reshape(T, E))

    gwT0 = np.ascontiguousarray(inp["L0_ffn_gate_w"].T.astype(np.float32))
    maps = []
    for c in range(NCORE):
        maps.append({"x": shard_rows(x_flat)[c], "at": shard_rows(attn_flat)[c],
                     "Wo": inp["L0_attn_Wo"], "bo": inp["L0_attn_bo"], "gwT": gwT0,
                     "ident": ident, "ones1": ones1})
    r = run_launch(build_a3, maps)
    x1_sh = [rr["x1"] for rr in r]
    ln2T_all = np.concatenate([rr["lnTo"] for rr in r], axis=2)
    cw_all = np.concatenate([rr["cwo"] for rr in r], axis=0)

    maps = []
    for c in range(NCORE):
        maps.append({"w1": np.ascontiguousarray(inp["L0_ffn_w1"][c].astype(np.float32)),
                     "w2": np.ascontiguousarray(inp["L0_ffn_w2"][c].astype(np.float32)),
                     "lnTd": ln2T_all, "cwc": np.ascontiguousarray(cw_all[:, c])})
    r = run_launch(build_b1, maps)
    y_ex = [rr["yo"] for rr in r]

    sw1T = np.ascontiguousarray(inp["L0_ffn_sw1"].T.astype(np.float32))
    sw2T = np.ascontiguousarray(inp["L0_ffn_sw2"].T.astype(np.float32))
    sw3T = np.ascontiguousarray(inp["L0_ffn_sw3"].T.astype(np.float32))
    maps = []
    for c in range(NCORE):
        maps.append({"x1": x1_sh[c],
                     "lnTd": np.ascontiguousarray(ln2T_all[:, :, c * NTOK:(c + 1) * NTOK]),
                     "ye": np.ascontiguousarray(np.stack([y_ex[e][c * NTOK:(c + 1) * NTOK] for e in range(8)])),
                     "sw1T": sw1T, "sw2T": sw2T, "sw3T": sw3T, "ident": ident})
    r = run_launch(build_b2, maps)
    x2_sh = [rr["x2"] for rr in r]

    qkvT = np.ascontiguousarray(inp["L1_attn_qkv_w"].T.astype(np.float32))
    maps = [{"x": x2_sh[c], "wT": qkvT, "ident": ident} for c in range(NCORE)]
    r = run_launch(build_c1, maps)
    qkv_all = np.concatenate([rr["qkv"] for rr in r], axis=0).reshape(2, S, 32, 192)
    ln3T_sh = [rr["lnTo"] for rr in r]

    qd_pc, kd_pc, ddT, bd = decay_buffers()
    maps = []
    for c in range(NCORE):
        idxs = list(range(c * 8, c * 8 + 8))
        hs = [i % 32 for i in idxs]
        maps.append({"qp": np.ascontiguousarray(np.stack([qkv_all[i // 32, :, i % 32, 0:64] for i in idxs])),
                     "kp": np.ascontiguousarray(np.stack([qkv_all[i // 32, :, i % 32, 64:128] for i in idxs])),
                     "vp": np.ascontiguousarray(np.stack([qkv_all[i // 32, :, i % 32, 128:192] for i in idxs])),
                     "ddT": np.ascontiguousarray(ddT[hs]), "qd": np.ascontiguousarray(qd_pc[hs]),
                     "kd": np.ascontiguousarray(kd_pc[hs]),
                     "bdc": np.ascontiguousarray(np.repeat(bd[hs][:, None], 64, axis=1)),
                     "ident": ident})
    r = run_launch(build_c2, maps)
    at1 = np.empty((2, S, 32, 64), dtype=NPBF16)
    for c in range(NCORE):
        for j, i in enumerate(range(c * 8, c * 8 + 8)):
            at1[i // 32, :, i % 32, :] = r[c]["op"][j]
    attn1_flat = np.ascontiguousarray(at1.reshape(T, E))

    gwT1 = np.ascontiguousarray(inp["L1_attn_gate_w"].T.astype(np.float32))
    owT1 = np.ascontiguousarray(inp["L1_attn_out_w"].T.astype(np.float32))
    maps = []
    for c in range(NCORE):
        maps.append({"x": x2_sh[c], "at": shard_rows(attn1_flat)[c], "lnTd": ln3T_sh[c],
                     "nw": inp["L1_attn_norm_w"].astype(np.float32), "gwT": gwT1, "owT": owT1,
                     "ident": ident, "ones1": ones1})
    r = run_launch(build_c3, maps)
    x3_sh = [rr["x3"] for rr in r]

    maps = []
    for c in range(NCORE):
        maps.append({"x": x3_sh[c], "wg": inp["L1_ffn_w_gate"], "wu": inp["L1_ffn_w_up"],
                     "wd": inp["L1_ffn_w_down"], "sg": inp["L1_ffn_s_gate"].astype(np.float32),
                     "su": inp["L1_ffn_s_up"].astype(np.float32), "sd": inp["L1_ffn_s_down"].astype(np.float32),
                     "ident": ident, "ones1": ones1})
    r = run_launch(build_d1, maps)
    out = np.concatenate([rr["x4"] for rr in r], axis=0)
    return out.reshape(2, S, E).astype(np.float32)
